# version 26
# baseline (speedup 1.0000x reference)
import numpy as np
import sys
from functools import reduce
from contextlib import ExitStack

import concourse.bass as bass
import concourse.mybir as mybir
from concourse.bass_utils import run_bass_kernel_spmd

F32 = mybir.dt.float32
BF16 = mybir.dt.bfloat16
AF = mybir.ActivationFunctionType
ALU = mybir.AluOpType
AX = mybir.AxisListType

B, S, D = 4, 8192, 1024
NCORE = 8
TOWN = 4096
HALO = 64
TEXT = TOWN + 2 * HALO
NT = TEXT // 128
NIC = D // 128
DH = 4096
NHC = DH // 128
NW = 65
GT = 256
GTE = GT + 64
NGRP = 17
EPS = 1e-5
TPMAX = 512
NTF = 32
FFN_PASSES = [list(range(4 * p, 4 * p + 4)) for p in range(8)]
AF_COPY = AF.Identity


def prod(xs):
    return reduce(lambda a, b: a * b, xs)


def rum(x, m):
    return (x + m - 1) // m * m


class Arena:
    def __init__(self, nc, size):
        self.nc = nc
        self.size = size
        self.handle = nc.alloc_sbuf_tensor("arena", [128, size], mybir.dt.uint8)
        self.base = nc.lookup_mloc(self.handle).addr
        self.top = 0
        self.hi = 0
        self._offs = {}

    def al(self, name, shape, dtype, at=None):
        per = prod(shape[1:]) * mybir.dt.size(dtype)
        off = self.top if at is None else at
        h = self.nc.alloc_sbuf_tensor_at(name, list(shape), dtype,
                                         offset=self.base + off)
        self._offs[id(h)] = off
        end = rum(off + per, 32)
        if at is None:
            self.top = end
        self.hi = max(self.hi, end)
        assert self.hi <= self.size, (name, self.hi, self.size)
        return h

    def mark(self):
        return self.top

    def reset(self, p):
        self.top = p

    def offset_of(self, h):
        return self._offs[id(h)]


class Ev:
    def __init__(self, h):
        self.h = h
        self.n = 0


class DmaPool:

    def __init__(self, sems):
        self.sems = sems
        self.i = 0


class Sched:
    def __init__(self):
        self.items = []

    def emit(self, eng, fn, waits=(), inc=None, inc_by=1):
        f = sys._getframe(1)
        self._tags = getattr(self, "_tags", [])
        self._tags.append((f.f_lineno, f.f_back.f_lineno if f.f_back else 0))
        ws = {}
        for w in waits:
            if w is None:
                continue
            ev, v = w
            if v <= 0:
                continue
            k = id(ev)
            ws[k] = (ev.h, max(v, ws[k][1]) if k in ws else v)
        if isinstance(inc, DmaPool):
            ev = inc.sems[inc.i % len(inc.sems)]
            inc.i += 1
            if ev.n > 0:
                ws[id(ev)] = (ev.h, ev.n)
            inc = ev
        rec = None
        if inc is not None:
            inc.n += inc_by
            rec = (inc, inc.n)
        self.items.append((eng, fn, list(ws.values()),
                           None if inc is None else (inc.h, inc_by)))
        return rec

    def run(self, name, eng):
        names = name if isinstance(name, (set, frozenset)) else {name}
        for en, fn, waits, inc in self.items:
            if en not in names:
                continue
            for h, v in waits:
                eng.wait_ge(h, v)
            inst = fn(eng) if fn is not None else None
            if inc is not None:
                h, n = inc
                inst.then_inc(h, n)


def windows_of_group():
    out = []
    last = -1
    for g in range(NGRP):
        hi = min(NW - 1, 4 * g + 2) if g < 16 else NW - 1
        out.append(list(range(last + 1, hi + 1)))
        last = hi
    return out


def out_tiles_of_group():
    out = [[0]]
    for g in range(1, 16):
        out.append([2 * g - 1, 2 * g])
    out.append([31, 32])
    return out


def qk_slice(j):
    g = j // 4
    return g, 64 * j - GT * g


def build_nc(trunc=None):
    nc = bass.Bass("TRN2", target_bir_lowering=False)

    x_ext = nc.dram_tensor("x_ext", [TEXT, D], F32, kind="ExternalInput").ap()
    wvec_in = nc.dram_tensor("wvec", [128, NW], F32, kind="ExternalInput").ap()
    w_in = {k: nc.dram_tensor(k, [128, NIC * D], F32, kind="ExternalInput").ap()
            for k in ("wq", "wk", "wv", "wo")}
    bq_in = nc.dram_tensor("bq", [128, NIC], F32, kind="ExternalInput").ap()
    bk_in = nc.dram_tensor("bk", [128, NIC], F32, kind="ExternalInput").ap()
    bv_in = nc.dram_tensor("bvb", [128, D], F32, kind="ExternalInput").ap()
    bo_in = nc.dram_tensor("bob", [128, D], F32, kind="ExternalInput").ap()
    s1_in = nc.dram_tensor("s1", [128, NHC], F32, kind="ExternalInput").ap()
    s2_in = nc.dram_tensor("s2", [128, NHC], F32, kind="ExternalInput").ap()
    sh1_in = nc.dram_tensor("sh1", [128, NHC], F32, kind="ExternalInput").ap()
    sh2_in = nc.dram_tensor("sh2", [128, NHC], F32, kind="ExternalInput").ap()
    s3_in = nc.dram_tensor("s3", [128, NIC], F32, kind="ExternalInput").ap()
    w12b = nc.dram_tensor("w12b", [NHC, 128, 2 * NIC * 128], F32,
                          kind="ExternalInput").ap()
    h1b = nc.dram_tensor("h1b", [NHC, 128, NHC * 128], F32, kind="ExternalInput").ap()
    h2b = nc.dram_tensor("h2b", [NHC, 128, NHC * 128], F32, kind="ExternalInput").ap()
    w3b = nc.dram_tensor("w3b", [NIC, 128, NHC * 128], F32, kind="ExternalInput").ap()
    out_h = nc.dram_tensor("out", [TOWN, D], F32, kind="ExternalOutput").ap()
    w12s = nc.dram_tensor("w12s", [NHC, 128, 2 * NIC * 128], BF16).ap()
    h1s = nc.dram_tensor("h1s", [NHC, 128, NHC * 128], BF16).ap()
    h2s = nc.dram_tensor("h2s", [NHC, 128, NHC * 128], BF16).ap()
    w3s = nc.dram_tensor("w3s", [NIC, 128, NHC * 128], BF16).ap()
    xms = nc.dram_tensor("xms", [TEXT, D], F32).ap()

    ar = Arena(nc, nc.sbuf_bytes_remaining - 1024)
    wvec = ar.al("p_wvec", [128, NW], F32)
    bq8 = ar.al("p_bq8", [128, NIC], F32)
    bk_s = ar.al("p_bk", [128, NIC], F32)
    bv_b = ar.al("p_bvb", [128, D], F32)
    bo_b = ar.al("p_bob", [128, D], F32)
    s1_s = ar.al("p_s1", [128, NHC], F32)
    s2_s = ar.al("p_s2", [128, NHC], F32)
    sh1_s = ar.al("p_sh1", [128, NHC], F32)
    sh2_s = ar.al("p_sh2", [128, NHC], F32)
    s3_s = ar.al("p_s3", [128, NIC], F32)
    id_f = ar.al("p_idf", [128, 128], F32)
    w12r = [ar.al(f"p_w12r{i}", [128, 2 * NIC * 128], BF16) for i in range(2)]
    wbig = [ar.al(f"p_wbig{i}", [128, NHC * 128], BF16) for i in range(2)]
    pmark = ar.mark()

    wq_s = ar.al("a_wq", [128, NIC * D], BF16)
    wk_s = ar.al("a_wk", [128, NIC * D], BF16)
    wv_s = ar.al("a_wv", [128, NIC * D], BF16)
    wo_s = ar.al("a_wo", [128, NIC * D], BF16)
    xr = [ar.al(f"a_x{i}", [128, D], F32) for i in range(7)]
    lntm = [ar.al(f"a_lntm{i}", [128, D], F32) for i in range(2)]
    lnfm = [ar.al(f"a_lnfm{i}", [128, NIC * GTE], BF16) for i in range(2)]
    qTb = [ar.al(f"a_qT{i}", [128, NIC * GTE], BF16) for i in range(2)]
    kTb = [ar.al(f"a_kT{i}", [128, NIC * GTE], BF16) for i in range(2)]
    vb = [ar.al(f"a_v{i}", [128, 2 * D], BF16) for i in range(2)]
    vsb = [ar.al(f"a_vs{i}", [128, D], BF16) for i in range(3)]
    outsb = [ar.al(f"a_out{i}", [128, D], BF16) for i in range(4)]
    probs = [ar.al(f"a_pr{i}", [128, 128], F32) for i in range(2)]
    attnT = [ar.al(f"a_aT{i}", [128, 128], BF16) for i in range(6)]
    attnsc = [ar.al(f"a_asc{i}", [128, 128], F32) for i in range(3)]
    xmst = [ar.al(f"a_xm{i}", [128, D], F32) for i in range(2)]
    wotmp = [ar.al(f"a_wot{i}", [128, D], F32) for i in range(2)]
    st12 = [ar.al(f"a_st12_{i}", [128, 12], F32) for i in range(2)]
    mvr = [ar.al(f"a_mv{i}", [128, 2], F32) for i in range(4)]
    rstdr = [ar.al(f"a_rstd{i}", [128, 1], F32) for i in range(4)]
    lnv = [ar.al(f"a_lnv{i}", [128, 1], F32) for i in range(2)]
    nmx = [ar.al(f"a_nmx{i}", [128, 1], F32) for i in range(2)]
    rsum = [ar.al(f"a_rs{i}", [128, 1], F32) for i in range(2)]
    rinv = [ar.al(f"a_ri{i}", [128, 1], F32) for i in range(2)]
    mfac = [ar.al(f"a_mf{i}", [128, 1], F32) for i in range(2)]
    veps = [ar.al(f"a_veps{i}", [128, 1], F32) for i in range(2)]
    attn_end = ar.mark()

    ar.reset(pmark)
    xmr = [ar.al(f"f_xm{i}", [128, D], F32) for i in range(10)]
    flntm = [ar.al(f"f_lntm{i}", [128, D], F32) for i in range(2)]
    flnfm = ar.al("f_lnfm", [128, NIC * TPMAX], BF16)
    aT = ar.al("f_aT", [128, NHC * TPMAX], BF16)
    bT = ar.al("f_bT", [128, NHC * TPMAX], BF16)
    out3T = ar.al("f_o3T", [128, NIC * TPMAX], F32, at=ar.offset_of(bT))
    usb = [ar.al(f"f_u{i}", [128, TPMAX], F32) for i in range(2)]
    ost = [ar.al(f"f_ost{i}", [128, D], F32) for i in range(2)]
    fst = [ar.al(f"f_st{i}", [128, 12], F32) for i in range(2)]
    fmv = [ar.al(f"f_mv{i}", [128, 2], F32) for i in range(2)]
    fveps = [ar.al(f"f_veps{i}", [128, 1], F32) for i in range(2)]
    flnv = [ar.al(f"f_lnv{i}", [128, 1], F32) for i in range(2)]
    frstd = [ar.al(f"f_rstd{i}", [128, 1], F32) for i in range(2)]
    ffn_end = ar.mark()

    cT = aT

    Z = [nc.alloc_psum_tensor(f"Z{i}", [128, 1024], F32) for i in range(4)]
    BANKS = [(Z[0], 0), (Z[0], 512), (Z[1], 0), (Z[1], 512)]
    SCORES = [Z[0][:, 0:128], Z[1][:, 512:640]]

    sched = Sched()
    ctx = ExitStack()
    pe = Ev(ctx.enter_context(nc.semaphore("s_pe")))
    dv = Ev(ctx.enter_context(nc.semaphore("s_dv")))
    ac = Ev(ctx.enter_context(nc.semaphore("s_ac")))
    po = Ev(ctx.enter_context(nc.semaphore("s_po")))
    gd = DmaPool([Ev(ctx.enter_context(nc.semaphore(f"s_g{i}")))
                  for i in range(32)])
    sd = DmaPool([Ev(ctx.enter_context(nc.semaphore(f"s_y{i}")))
                  for i in range(16)])
    E = sched.emit

    xle = [None] * NT
    xfree = [None] * 7

    def load_x(t):
        xle[t] = E("gp", lambda e, t=t: e.dma_start(
            out=xr[t % 7][:], in_=x_ext[t * 128:(t + 1) * 128, :]),
            waits=[xfree[t % 7]], inc=gd, inc_by=16)

    wcast = {}

    def cast_w(name, dst):
        ev = E("gp", lambda e, d=dst, s=w_in[name]:
               e.dma_start(out=d[:], in_=s), inc=gd, inc_by=16)
        wcast[name] = (ev,)

    load_x(0)
    load_x(1)
    msev = E("gp", lambda e: e.memset(id_f[:], 0.0), inc=po)
    idev = E("gp", lambda e: e.affine_select(
        out=id_f[:], in_=id_f[:], compare_op=ALU.not_equal, fill=1.0, base=0,
        pattern=[[-1, 128]], channel_multiplier=1), waits=[msev], inc=po)
    sm = E("gp", lambda e: e.dma_start(out=bq8[:], in_=bq_in),
           inc=gd, inc_by=16)
    small_done = sm
    bq8ev = E("dv", lambda e: e.tensor_scalar_mul(bq8[:], bq8[:], 0.125),
              waits=[small_done], inc=dv)
    cast_w("wq", wq_s)
    cast_w("wk", wk_s)
    cast_w("wv", wv_s)
    load_x(2)
    load_x(3)
    cast_w("wo", wo_s)
    ld_ev = {}
    for nm, dst, src in [("wvec", wvec, wvec_in), ("bk", bk_s, bk_in),
                         ("bvb", bv_b, bv_in)]:
        ld_ev[nm] = E("gp", lambda e, d=dst, s=src:
                      e.dma_start(out=d[:], in_=s), inc=gd, inc_by=16)
    for nm, dst, src in [("bob", bo_b, bo_in), ("s1", s1_s, s1_in),
                         ("s2", s2_s, s2_in), ("sh1", sh1_s, sh1_in),
                         ("sh2", sh2_s, sh2_in), ("s3", s3_s, s3_in)]:
        ld_ev[nm] = E("gp", lambda e, d=dst, s=src:
                      e.dma_start(out=d[:], in_=s), inc=gd, inc_by=16)

    conv_jobs = []
    for name, src, dst, nb, step in (("w12", w12b, w12s, NHC, 8),
                                     ("h1", h1b, h1s, NHC, 4),
                                     ("h2", h2b, h2s, NHC, 4),
                                     ("w3", w3b, w3s, NIC, 4)):
        for lo in range(0, nb, step):
            conv_jobs.append((name, src, dst, lo, min(lo + step, nb), step))
    conv_ev = {}

    def emit_conv(job):
        name, src, dst, lo, hi, step = job
        ev = E("gp", lambda e, src=src, dst=dst, lo=lo, hi=hi:
               e.dma_start(out=dst[lo:hi], in_=src[lo:hi]), inc=gd, inc_by=16)
        conv_ev[(name, lo // step)] = ev

    for job in list(conv_jobs):
        if job[0] == "w12":
            emit_conv(job)
            conv_jobs.remove(job)

    WOG = windows_of_group()
    OTG = out_tiles_of_group()
    bank_free = [None] * 4
    P4 = [1, 2, 0, 3]
    P2 = [1, 2]
    p4_i = [0]
    p2_i = [0]

    def next4():
        b = P4[p4_i[0] % 4]
        p4_i[0] += 1
        return b

    def next2():
        b = P2[p2_i[0] % 2]
        p2_i[0] += 1
        return b
    zout_free = [None, None]
    lnfm_ev = [None, None]
    qk_ev = [None, None]
    v_ev = [None, None]
    outsb_ev = [None] * 4
    outsb_free = [None] * 4
    attnT_ev = [None] * 6
    attnT_free = [None] * 6
    probs_free = [None] * 2
    attnsc_free = [None] * 3
    asc_ev = [None] * 3
    vs_ev = [None] * 3
    vs_free = [None] * 3
    xm_free = [None] * 2
    exp_prev = [None]
    wotmp_free = [None] * 2
    scores_last_qk = {}
    out_last_v = {}
    lnfm_last_pe = {}
    xmid_spill = [None] * NT

    def next_proj():
        i = proj_i[0] % 2
        proj_i[0] += 1
        return i

    norm_ev = [None] * NT
    tptile_ev = [None] * NT

    def ln_stats(t):
        sl12 = st12[t % 2]
        mv = mvr[t % 4]
        rst = rstdr[t % 4]
        e1 = E("dv", lambda e, t=t, sl12=sl12: e.bn_stats(
            sl12[:, 0:6], xr[t % 7][:, 0:512]), waits=[xle[t]], inc=dv)
        e2 = E("dv", lambda e, t=t, sl12=sl12: e.bn_stats(
            sl12[:, 6:12], xr[t % 7][:, 512:1024]), waits=[e1], inc=dv)
        ea = E("dv", lambda e, mv=mv, sl12=sl12: e.bn_aggr(mv[:], sl12[:]),
               waits=[e2], inc=dv)
        ep = E("dv", lambda e, t=t, mv=mv: e.tensor_scalar_add(
            veps[t % 2][:], mv[:, 1:2], EPS), waits=[ea], inc=dv)
        el = E("ac", lambda e, t=t: e.activation(
            lnv[t % 2][:], veps[t % 2][:], AF.Ln, scale=1.0),
            waits=[ep], inc=ac)
        er = E("ac", lambda e, t=t, rst=rst: e.activation(
            rst[:], lnv[t % 2][:], AF.Exp, bias=0.0, scale=-0.5),
            waits=[el], inc=ac)
        norm_ev[t] = E("dv", lambda e, t=t, mv=mv, rst=rst: e.tensor_scalar(
            lntm[t % 2][:], xr[t % 7][:], mv[:, 0:1], rst[:],
            ALU.subtract, ALU.mult),
            waits=[er, tptile_ev[t - 2] if t >= 2 else None], inc=dv)

    def ln_transposes(t, g, lnfm_war):
        last = None
        for c in range(NIC):
            b = next4()
            zt, zo = BANKS[b]
            sl = zt[:, zo:zo + 128]
            it = (t - 2 * g) if g < 16 else 0
            tp = E("pe", lambda e, t=t, c=c, sl=sl: e.transpose(
                sl, lntm[t % 2][:, 128 * c:128 * c + 128], id_f[:]),
                waits=[norm_ev[t], idev, bank_free[b]], inc=pe)
            tptile_ev[t] = tp
            ev = E("dv", lambda e, g=g, c=c, it=it, sl=sl: e.tensor_copy(
                lnfm[g % 2][:, c * GTE + it * 128:c * GTE + it * 128 + 128], sl),
                waits=[tp, lnfm_war if (c == 0 and it == 0) else None], inc=dv)
            if it == 0 and g >= 1:
                ev = E("dv", lambda e, g=g, c=c, sl=sl: e.tensor_copy(
                    lnfm[(g - 1) % 2][:, c * GTE + GT:c * GTE + GT + 64],
                    sl[:, 0:64]), waits=[], inc=dv)
            bank_free[b] = ev
            last = ev
        return last

    def window_scores(j):
        wg, off = qk_slice(j)
        mmlast = None
        for ic in range(NIC):
            mmlast = E("pe", lambda e, ic=ic, wg=wg, off=off, j=j,
                       st=(ic == 0), sp=(ic == NIC - 1): e.matmul(
                SCORES[j % 2][:, 0:128],
                qTb[wg % 2][:, ic * GTE + off:ic * GTE + off + 128],
                kTb[wg % 2][:, ic * GTE + off:ic * GTE + off + 128],
                start=st, stop=sp),
                waits=([qk_ev[0], qk_ev[1], bank_free[0 if j % 2 == 0 else 3]]
                       if ic == 0 else []),
                inc=(pe if ic == NIC - 1 else None))
        scores_last_qk[wg] = mmlast
        nm = E("dv", lambda e, j=j: e.reduce_max(
            nmx[j % 2][:], SCORES[j % 2][:, 0:128], AX.X, negate=True),
            waits=[mmlast], inc=dv)
        ex = E("ac", lambda e, j=j: e.activation(
            probs[j % 2][:], SCORES[j % 2][:, 0:128], AF.Exp,
            bias=nmx[j % 2][:], scale=1.0, accum_out=rsum[j % 2][:]),
            waits=[nm, probs_free[j % 2]], inc=ac)
        bank_free[0 if j % 2 == 0 else 3] = ex
        ri = E("dv", lambda e, j=j: e.reciprocal(rinv[j % 2][:], rsum[j % 2][:]),
               waits=[ex], inc=dv)
        mf = E("dv", lambda e, j=j: e.tensor_mul(
            mfac[j % 2][:], rinv[j % 2][:], wvec[:, j:j + 1]),
            waits=[ri, ld_ev["wvec"]], inc=dv)
        asc = E("dv", lambda e, j=j: e.tensor_scalar_mul(
            attnsc[j % 3][:], probs[j % 2][:], mfac[j % 2][:]),
            waits=[mf, attnsc_free[j % 3]], inc=dv)
        probs_free[j % 2] = asc
        asc_ev[j % 3] = asc

    tp_ptr = [0]

    def window_transpose(j):
        b = next2()
        zt, zo = BANKS[b]
        tp = E("pe", lambda e, j=j, zt=zt, zo=zo: e.transpose(
            zt[:, zo:zo + 128], attnsc[j % 3][:], id_f[:]),
            waits=[asc_ev[j % 3], bank_free[b], idev], inc=pe)
        attnsc_free[j % 3] = tp
        at = E("dv", lambda e, j=j, zt=zt, zo=zo: e.tensor_copy(
            attnT[j % 6][:], zt[:, zo:zo + 128]),
            waits=[tp, attnT_free[j % 6]], inc=dv)
        bank_free[b] = at
        attnT_ev[j % 6] = at

    def transpose_upto(j):
        j = min(j, NW - 1)
        while tp_ptr[0] <= j:
            window_transpose(tp_ptr[0])
            tp_ptr[0] += 1

    def v_ap(u):
        return vb[(u // 2) % 2][:, (u % 2) * D:(u % 2) * D + D]

    def out_tile(u):
        zt = Z[2] if u % 2 == 0 else Z[3]
        jc, jl, jr = 2 * u, 2 * u - 1, 2 * u + 1
        have_l = jl >= 1
        have_r = jr <= NW - 1
        mms = []
        for c in range(NIC):
            mms.append((zt[:, 128 * c:128 * c + 128],
                        v_ap(u)[:, 128 * c:128 * c + 128],
                        attnT[jc % 6][:, 0:128], (c == 0 or c == 4)))
        if have_l:
            for c in range(NIC):
                mms.append((zt[:, 128 * c:128 * c + 64],
                            vsb[(jl // 2) % 3][:, 128 * c:128 * c + 128],
                            attnT[jl % 6][:, 64:128], False))
        if have_r:
            for c in range(NIC):
                mms.append((zt[:, 128 * c + 64:128 * c + 128],
                            vsb[(jr // 2) % 3][:, 128 * c:128 * c + 128],
                            attnT[jr % 6][:, 0:64], False))
        w0 = [attnT_ev[jc % 6], zout_free[u % 2], v_ev[(u // 2) % 2]]
        if have_l:
            w0 += [attnT_ev[jl % 6], vs_ev[(jl // 2) % 3]]
        if have_r:
            w0 += [attnT_ev[jr % 6], vs_ev[(jr // 2) % 3]]
        mmlast = None
        for i, (o, l, r, st) in enumerate(mms):
            mmlast = E("pe", lambda e, o=o, l=l, r=r, st=st,
                       sp=(i == len(mms) - 1): e.matmul(
                o, l, r, start=st, stop=sp, skip_group_check=True),
                waits=(w0 if i == 0 else []),
                inc=(pe if i == len(mms) - 1 else None))
        for gg in {(u - 1) // 2, u // 2, (u + 1) // 2}:
            if gg >= 0:
                out_last_v[gg] = mmlast
        if have_l:
            vs_free[(jl // 2) % 3] = mmlast
        if have_r:
            vs_free[(jr // 2) % 3] = mmlast
        ev = E("ac", lambda e, u=u, zt=zt: e.copy(outsb[u % 4][:], zt[:, 0:D]),
               waits=[mmlast, outsb_free[u % 4]], inc=ac)
        zout_free[u % 2] = ev
        outsb_ev[u % 4] = ev
        attnT_free[jc % 6] = mmlast
        if have_l:
            attnT_free[jl % 6] = mmlast
        if have_r:
            attnT_free[jr % 6] = mmlast

    def wo_tile(u):
        ev = None
        for oh in range(2):
            pi = next4()
            zt, zo = BANKS[pi]
            mmlast = None
            for ic in range(NIC):
                mmlast = E("pe", lambda e, ic=ic, oh=oh, zt=zt, zo=zo, u=u: e.matmul(
                    zt[:, zo:zo + 512],
                    outsb[u % 4][:, ic * 128:ic * 128 + 128],
                    wo_s[:, ic * D + oh * 512:ic * D + oh * 512 + 512],
                    start=(ic == 0), stop=(ic == NIC - 1)),
                    waits=([outsb_ev[u % 4], *wcast["wo"], bank_free[pi]]
                           if ic == 0 else []),
                    inc=(pe if ic == NIC - 1 else None))
            ev = E("dv", lambda e, oh=oh, zt=zt, zo=zo, u=u: e.scalar_tensor_tensor(
                wotmp[u % 2][:, oh * 512:oh * 512 + 512], zt[:, zo:zo + 512],
                1.0, bo_b[:, oh * 512:oh * 512 + 512], ALU.mult, ALU.add),
                waits=[mmlast, ld_ev["bob"],
                       wotmp_free[u % 2] if oh == 0 else None], inc=dv)
            bank_free[pi] = ev
        outsb_free[u % 4] = ev
        pl = E("po", lambda e, u=u: e.tensor_add(
            xmst[u % 2][:], wotmp[u % 2][:], xr[u % 7][:]),
            waits=[ev, xle[u],
                   xmid_spill[u - 2] if u >= 2 else None], inc=po)
        wotmp_free[u % 2] = pl
        xfree[u % 7] = pl
        sp = E("gp", lambda e, u=u: e.dma_start(
            out=xms[u * 128:(u + 1) * 128, :], in_=xmst[u % 2][:]),
            waits=[pl], inc=gd, inc_by=16)
        xmid_spill[u] = sp
        xm_free[u % 2] = sp

    def attn_group(g):
        tiles = [2 * g, 2 * g + 1] if g < 16 else [32]
        TT = 128 * len(tiles)
        for _ in range(2):
            if conv_jobs:
                emit_conv(conv_jobs.pop(0))
        prev_wo = OTG[g - 1] if g >= 1 else []
        if prev_wo:
            wo_tile(prev_wo[0])
        qk_war = scores_last_qk.get(g - 2)
        ev = None
        for kind, wsb, bias_ap, scale, dstb in (
                ("q", wq_s, bq8, 0.125, qTb), ("k", wk_s, bk_s, 1.0, kTb)):
            wev = wcast["w" + kind]
            for oc in range(NIC):
                pi = next4()
                zt, zo = BANKS[pi]
                mmlast = None
                for ic in range(NIC):
                    mmlast = E("pe", lambda e, ic=ic, oc=oc, zt=zt, zo=zo,
                               w=wsb, g=g, TT=TT: e.matmul(
                        zt[:, zo:zo + TT],
                        w[:, ic * D + oc * 128:ic * D + oc * 128 + 128],
                        lnfm[g % 2][:, ic * GTE:ic * GTE + TT],
                        start=(ic == 0), stop=(ic == NIC - 1)),
                        waits=([lnfm_ev[g % 2], *wev, bank_free[pi],
                                qk_war if oc == 0 else None]
                               if ic == 0 else []),
                        inc=(pe if ic == NIC - 1 else None))
                ev = E("dv", lambda e, oc=oc, zt=zt, zo=zo, b=bias_ap,
                       sc=scale, db=dstb, g=g, TT=TT: e.tensor_scalar(
                    db[g % 2][:, oc * GTE:oc * GTE + TT], zt[:, zo:zo + TT],
                    sc, b[:, oc:oc + 1], ALU.mult, ALU.add),
                    waits=[mmlast, bq8ev, ld_ev["bk"]], inc=dv)
                if g >= 1:
                    ev = E("dv", lambda e, oc=oc, zt=zt, zo=zo, b=bias_ap,
                           sc=scale, db=dstb, g=g: e.tensor_scalar(
                        db[(g - 1) % 2][:, oc * GTE + GT:oc * GTE + GT + 64],
                        zt[:, zo:zo + 64],
                        sc, b[:, oc:oc + 1], ALU.mult, ALU.add),
                        waits=[], inc=dv)
                bank_free[pi] = ev
        qk_ev[g % 2] = ev
        for u in prev_wo[1:]:
            wo_tile(u)
        v_war = out_last_v.get(g - 2)

        def v_proj():
          for it, t in enumerate(tiles):
            for oh in range(2):
                pi = next4()
                zt, zo = BANKS[pi]
                mmlast = None
                for ic in range(NIC):
                    mmlast = E("pe", lambda e, ic=ic, it=it, oh=oh, zt=zt,
                               zo=zo, g=g: e.matmul(
                        zt[:, zo:zo + 512],
                        lnfm[g % 2][:, ic * GTE + it * 128:ic * GTE + it * 128 + 128],
                        wv_s[:, ic * D + oh * 512:ic * D + oh * 512 + 512],
                        start=(ic == 0), stop=(ic == NIC - 1)),
                        waits=([lnfm_ev[g % 2], *wcast["wv"], bank_free[pi],
                                v_war if (it == 0 and oh == 0) else None]
                               if ic == 0 else []),
                        inc=(pe if ic == NIC - 1 else None))
                ve = E("dv", lambda e, it=it, oh=oh, zt=zt, zo=zo, g=g: e.tensor_tensor(
                    vb[g % 2][:, it * D + oh * 512:it * D + oh * 512 + 512],
                    zt[:, zo:zo + 512], bv_b[:, oh * 512:oh * 512 + 512],
                    ALU.add), waits=[mmlast, ld_ev["bvb"]], inc=dv)
                bank_free[pi] = ve
          v_ev[g % 2] = ve

        def vs_proj():
          for w in ([4 * (g - 1) + 3] if g >= 1 else []) + \
                ([4 * g + 1] if (g < 16 and 4 * g + 1 <= NW - 1) else []):
            wg2 = w // 4
            off = 64 * w - GT * wg2
            slot = (w // 2) % 3
            for oh in range(2):
                pi = next4()
                zt, zo = BANKS[pi]
                mmlast = None
                for ic in range(NIC):
                    mmlast = E("pe", lambda e, ic=ic, oh=oh, zt=zt, zo=zo,
                               wg2=wg2, off=off: e.matmul(
                        zt[:, zo:zo + 512],
                        lnfm[wg2 % 2][:, ic * GTE + off:ic * GTE + off + 128],
                        wv_s[:, ic * D + oh * 512:ic * D + oh * 512 + 512],
                        start=(ic == 0), stop=(ic == NIC - 1)),
                        waits=([lnfm_ev[g % 2], *wcast["wv"], bank_free[pi],
                                vs_free[slot] if oh == 0 else None]
                               if ic == 0 else []),
                        inc=(pe if ic == NIC - 1 else None))
                ve2 = E("dv", lambda e, oh=oh, zt=zt, zo=zo, slot=slot: e.tensor_tensor(
                    vsb[slot][:, oh * 512:oh * 512 + 512],
                    zt[:, zo:zo + 512], bv_b[:, oh * 512:oh * 512 + 512],
                    ALU.add), waits=[mmlast, ld_ev["bvb"]], inc=dv)
                bank_free[pi] = ve2
            vs_ev[slot] = ve2

        wjs = WOG[g]
        for idx, j in enumerate(wjs):
            window_scores(j)
            if idx == min(1, len(wjs) - 1):
                v_proj()
                transpose_upto(j - 1)
        vs_proj()
        lnfm_last_pe[g] = (pe, pe.n)
        transpose_upto(wjs[-1] - 1)
        if OTG[g]:
            transpose_upto(2 * OTG[g][-1] + 1)
        for u in OTG[g]:
            out_tile(u)
        if g + 1 <= 16:
            lnfm_war = scores_last_qk.get(g - 1)
            evl = None
            for t in ([2 * (g + 1), 2 * (g + 1) + 1] if g + 1 < 16 else [32]):
                evl = ln_transposes(t, g + 1, lnfm_war)
            lnfm_ev[(g + 1) % 2] = evl
        gg = g + 2
        if gg <= 16:
            for t in ([2 * gg, 2 * gg + 1] if gg < 16 else [32]):
                load_x(t)
            for t in ([2 * gg, 2 * gg + 1] if gg < 16 else [32]):
                ln_stats(t)

    sched.marks = {"setup": len(sched.items)}
    for t in (0, 1):
        ln_stats(t)
    evl = None
    for t in (0, 1):
        evl = ln_transposes(t, 0, None)
    lnfm_ev[0] = evl
    for t in (2, 3):
        ln_stats(t)
    for g in range(NGRP):
        attn_group(g)
        sched.marks[f"g{g}"] = len(sched.items)
    for u in OTG[16]:
        wo_tile(u)
    sched.marks["attn"] = len(sched.items)

    bar = ([(pe, pe.n), (dv, dv.n), (ac, ac.n), (po, po.n)] +
           [(s, s.n) for s in gd.sems if s.n > 0])
    for en in ("pe", "dv", "ac", "po", "gp"):
        E(en, None, waits=bar)

    zfree = {0: None, 1: None, 2: None, 3: None}
    w12_free = [None] * 2
    wbig_free = [None] * 2
    wbig_i = [0]
    xmr_free = [None] * 10
    flnfm_last_pe = [None]
    aT_ready = [None]
    bT_ready = [None]
    cT_ready = [None]
    flnfm_ev_c = [None] * NIC
    aT_ev_c = [None] * NHC
    bT_ev_c = [None] * NHC
    cT_ev_c = [None] * NHC
    o3_ready = [None]
    bT_last_pe = [None]
    aT_last_pe = [None]
    usb_free = [None, None]
    ost_free = [None, None]
    reload_ev = [None] * NTF
    flntm_free = [None, None]
    fz4 = [None] * 4
    f_el_ring = [None, None]
    f_en_ring = [None, None]
    w12_pref = {}

    def emit_reload(t):
        if reload_ev[t] is None:
            r = t % 10
            reload_ev[t] = E("gp", lambda e, t=t, r=r: e.dma_start(
                out=xmr[r][:], in_=xms[HALO + t * 128:HALO + (t + 1) * 128, :]),
                waits=[xmid_spill[t], xmid_spill[t + 1], xmr_free[r]],
                inc=gd, inc_by=16)

    def ffn_stage_a(p):
        tiles = FFN_PASSES[p]
        if p >= 1:
            for sl in (0, 1):
                w12_pref[(p, sl)] = E("ac", lambda e, sl=sl: e.dma_start(
                    out=w12r[sl][:], in_=w12s[sl]),
                    waits=[conv_ev[("w12", 0)], w12_free[sl]],
                    inc=sd, inc_by=16)
        evl = None
        for k, t in enumerate(tiles):
            r = t % 10
            emit_reload(t)
            rl = reload_ev[t]
            s1 = E("dv", lambda e, r=r, k=k: e.bn_stats(
                fst[k % 2][:, 0:6], xmr[r][:, 0:512]), waits=[rl], inc=dv)
            s2 = E("dv", lambda e, r=r, k=k: e.bn_stats(
                fst[k % 2][:, 6:12], xmr[r][:, 512:1024]), waits=[s1], inc=dv)
            sa = E("dv", lambda e, k=k: e.bn_aggr(
                fmv[k % 2][:], fst[k % 2][:]), waits=[s2], inc=dv)
            sp_ = E("dv", lambda e, k=k: e.tensor_scalar_add(
                fveps[k % 2][:], fmv[k % 2][:, 1:2], EPS),
                waits=[sa, f_el_ring[k % 2]], inc=dv)
            el = E("ac", lambda e, k=k: e.activation(
                flnv[k % 2][:], fveps[k % 2][:], AF.Ln, scale=1.0),
                waits=[sp_], inc=ac)
            er = E("ac", lambda e, k=k: e.activation(
                frstd[k % 2][:], flnv[k % 2][:], AF.Exp, bias=0.0, scale=-0.5),
                waits=[el, f_en_ring[k % 2]], inc=ac)
            f_el_ring[k % 2] = el
            en = E("dv", lambda e, r=r, k=k: e.tensor_scalar(
                flntm[k % 2][:], xmr[r][:], fmv[k % 2][:, 0:1],
                frstd[k % 2][:], ALU.subtract, ALU.mult),
                waits=[er, flntm_free[k % 2]], inc=dv)
            f_en_ring[k % 2] = en
            tpf = None
            for c in range(NIC):
                zi = c % 2
                so = 512 * ((c // 2) % 2)
                sl = Z[zi][:, so:so + 128]
                tp = E("pe", lambda e, c=c, sl=sl, k=k: e.transpose(
                    sl, flntm[k % 2][:, 128 * c:128 * c + 128], id_f[:]),
                    waits=[en, zfree[zi] if (k == 0 and c < 4) else None,
                           fz4[c % 4]], inc=pe)
                ev = E("dv", lambda e, c=c, sl=sl, k=k: e.tensor_copy(
                    flnfm[:, c * TPMAX + k * 128:c * TPMAX + k * 128 + 128], sl),
                    waits=[tp, flnfm_last_pe[0] if (c == 0 and k == 0) else None],
                    inc=dv)
                zfree[zi] = ev
                fz4[c % 4] = ev
                evl = ev
                tpf = tp
                flnfm_ev_c[c] = ev
            flntm_free[k % 2] = tpf

    def ffn_pass(p, tiles):
        TP = 128 * len(tiles)
        N0 = min(512, TP)
        stt_ev = None
        for oc in range(NHC):
            sl1 = oc % 2
            sl2 = sl1
            d1 = w12_pref.pop((p, oc), None)
            if d1 is None:
                eng = "sy" if oc % 2 == 0 else "ac"
                d1 = E(eng, lambda e, oc=oc, sl=sl1: e.dma_start(
                    out=w12r[sl][:], in_=w12s[oc]),
                    waits=[conv_ev[("w12", oc // 8)], w12_free[sl1]],
                    inc=sd, inc_by=16)
            d2 = d1
            z1 = Z[oc % 2]
            z2 = Z[2 + oc % 2]
            mm1 = mm2 = None
            for ic in range(NIC):
                mm1 = E("pe", lambda e, ic=ic, z1=z1, sl=sl1, N0=N0: e.matmul(
                    z1[:, 0:N0], w12r[sl][:, ic * 128:ic * 128 + 128],
                    flnfm[:, ic * TPMAX:ic * TPMAX + N0],
                    start=(ic == 0), stop=(ic == NIC - 1)),
                    waits=([d1, zfree[oc % 2]] if ic == 0 else []) +
                          ([flnfm_ev_c[ic]] if oc < 2 else []),
                    inc=(pe if (ic == NIC - 1 and TP <= 512) else None))
                if TP > 512:
                    mm1 = E("pe", lambda e, ic=ic, z1=z1, sl=sl1, TP=TP: e.matmul(
                        z1[:, 512:TP], w12r[sl][:, ic * 128:ic * 128 + 128],
                        flnfm[:, ic * TPMAX + 512:ic * TPMAX + TP],
                        start=(ic == 0), stop=(ic == NIC - 1)),
                        inc=(pe if ic == NIC - 1 else None))
            for ic in range(NIC):
                mm2 = E("pe", lambda e, ic=ic, z2=z2, sl=sl2, N0=N0: e.matmul(
                    z2[:, 0:N0],
                    w12r[sl][:, 1024 + ic * 128:1024 + ic * 128 + 128],
                    flnfm[:, ic * TPMAX:ic * TPMAX + N0],
                    start=(ic == 0), stop=(ic == NIC - 1)),
                    waits=([d2, zfree[2 + oc % 2]] if ic == 0 else []) +
                          ([flnfm_ev_c[ic]] if oc < 2 else []),
                    inc=(pe if (ic == NIC - 1 and TP <= 512) else None))
                if TP > 512:
                    mm2 = E("pe", lambda e, ic=ic, z2=z2, sl=sl2, TP=TP: e.matmul(
                        z2[:, 512:TP],
                        w12r[sl][:, 1024 + ic * 128:1024 + ic * 128 + 128],
                        flnfm[:, ic * TPMAX + 512:ic * TPMAX + TP],
                        start=(ic == 0), stop=(ic == NIC - 1)),
                        inc=(pe if ic == NIC - 1 else None))
            w12_free[sl1] = mm2
            su = E("ac", lambda e, oc=oc, z1=z1, TP=TP: e.activation(
                usb[oc % 2][:, 0:TP], z1[:, 0:TP], AF.Silu,
                scale=s1_s[:, oc:oc + 1]),
                waits=[mm1, usb_free[oc % 2]], inc=ac)
            zfree[oc % 2] = su
            st = E("dv", lambda e, oc=oc, z2=z2, TP=TP: e.scalar_tensor_tensor(
                aT[:, oc * TPMAX:oc * TPMAX + TP], z2[:, 0:TP],
                s2_s[:, oc:oc + 1], usb[oc % 2][:, 0:TP], ALU.mult, ALU.mult),
                waits=[mm2, su, aT_last_pe[0] if oc == 0 else None], inc=dv)
            zfree[2 + oc % 2] = st
            usb_free[oc % 2] = st
            aT_ev_c[oc] = st
            stt_ev = st
        aT_ready[0] = stt_ev
        flnfm_last_pe[0] = mm2
        for stage, hsrc, hkey, src, dst, scl, rdy, wdep in (
                ("h1", h1s, "h1", aT, bT, sh1_s, bT_ready, aT_ready),
                ("h2", h2s, "h2", bT, cT, sh2_s, cT_ready, bT_ready)):
            gel = None
            for pc in range(NHC):
                wslot = wbig_i[0] % 2
                wbig_i[0] += 1
                eng = "sy" if pc % 2 == 0 else "ac"
                dd = E(eng, lambda e, pc=pc, hs=hsrc, ws=wslot: e.dma_start(
                    out=wbig[ws][:], in_=hs[pc]),
                    waits=[conv_ev[(hkey, pc // 4)], wbig_free[wslot]],
                    inc=sd, inc_by=16)
                zb = Z[pc % 2]
                mml = None
                cev = aT_ev_c if stage == "h1" else bT_ev_c
                for ic in range(NHC):
                    mml = E("pe", lambda e, ic=ic, zb=zb, ws=wslot, src=src,
                            N0=N0: e.matmul(
                        zb[:, 0:N0], wbig[ws][:, ic * 128:ic * 128 + 128],
                        src[:, ic * TPMAX:ic * TPMAX + N0],
                        start=(ic == 0), stop=(ic == NHC - 1)),
                        waits=([dd, zfree[pc % 2]] if ic == 0 else []) +
                              ([cev[ic]] if pc < 2 else []),
                        inc=(pe if (ic == NHC - 1 and TP <= 512) else None))
                    if TP > 512:
                        mml = E("pe", lambda e, ic=ic, zb=zb, ws=wslot,
                                src=src, TP=TP: e.matmul(
                            zb[:, 512:TP], wbig[ws][:, ic * 128:ic * 128 + 128],
                            src[:, ic * TPMAX + 512:ic * TPMAX + TP],
                            start=(ic == 0), stop=(ic == NHC - 1)),
                            inc=(pe if ic == NHC - 1 else None))
                wbig_free[wslot] = mml
                if stage == "h1":
                    aT_last_pe[0] = mml
                else:
                    bT_last_pe[0] = mml
                gel = E("ac", lambda e, pc=pc, zb=zb, dst=dst, scl=scl,
                        TP=TP: e.activation(
                    dst[:, pc * TPMAX:pc * TPMAX + TP], zb[:, 0:TP], AF.Gelu,
                    scale=scl[:, pc:pc + 1]),
                    waits=[mml, aT_last_pe[0] if (stage == "h2" and pc == 0)
                           else None], inc=ac)
                zfree[pc % 2] = gel
                (bT_ev_c if stage == "h1" else cT_ev_c)[pc] = gel
            rdy[0] = gel
        o3l = None
        for oc in range(NIC):
            wslot = wbig_i[0] % 2
            wbig_i[0] += 1
            eng = "sy" if oc % 2 == 0 else "ac"
            dd = E(eng, lambda e, oc=oc, ws=wslot: e.dma_start(
                out=wbig[ws][:], in_=w3s[oc]),
                waits=[conv_ev[("w3", oc // 4)], wbig_free[wslot]],
                inc=sd, inc_by=16)
            zb = Z[2 + oc % 2]
            mml = None
            for hc in range(NHC):
                mml = E("pe", lambda e, hc=hc, zb=zb, ws=wslot, N0=N0: e.matmul(
                    zb[:, 0:N0], wbig[ws][:, hc * 128:hc * 128 + 128],
                    cT[:, hc * TPMAX:hc * TPMAX + N0],
                    start=(hc == 0), stop=(hc == NHC - 1)),
                    waits=([dd, zfree[2 + oc % 2]] if hc == 0 else []) +
                          ([cT_ev_c[hc]] if oc < 2 else []),
                    inc=(pe if (hc == NHC - 1 and TP <= 512) else None))
                if TP > 512:
                    mml = E("pe", lambda e, hc=hc, zb=zb, ws=wslot, TP=TP: e.matmul(
                        zb[:, 512:TP], wbig[ws][:, hc * 128:hc * 128 + 128],
                        cT[:, hc * TPMAX + 512:hc * TPMAX + TP],
                        start=(hc == 0), stop=(hc == NHC - 1)),
                        inc=(pe if hc == NHC - 1 else None))
            wbig_free[wslot] = mml
            aT_last_pe[0] = mml
            o3l = E("dv", lambda e, oc=oc, zb=zb, TP=TP: e.tensor_scalar_mul(
                out3T[:, oc * TPMAX:oc * TPMAX + TP], zb[:, 0:TP],
                s3_s[:, oc:oc + 1]),
                waits=[mml, bT_last_pe[0] if oc == 0 else None], inc=dv)
            zfree[2 + oc % 2] = o3l
        o3_ready[0] = o3l
        if p + 1 < len(FFN_PASSES):
            ffn_stage_a(p + 1)
        for k, t in enumerate(tiles):
            zb = Z[2 + k % 2]
            tpl = None
            for c in range(NIC):
                tpl = E("pe", lambda e, c=c, zb=zb, k=k: e.transpose(
                    zb[:, 128 * c:128 * c + 128],
                    out3T[:, c * TPMAX + k * 128:c * TPMAX + k * 128 + 128],
                    id_f[:]),
                    waits=([o3_ready[0], zfree[2 + k % 2]] if c == 0 else []),
                    inc=pe)
            ad = E("dv", lambda e, t=t, k=k, zb=zb: e.tensor_add(
                ost[k % 2][:], zb[:, 0:D], xmr[t % 10][:]),
                waits=[tpl, ost_free[k % 2]], inc=dv)
            zfree[2 + k % 2] = ad
            stv = E("gp", lambda e, t=t, k=k: e.dma_start(
                out=out_h[t * 128:(t + 1) * 128, :], in_=ost[k % 2][:]),
                waits=[ad], inc=gd, inc_by=16)
            ost_free[k % 2] = stv
            xmr_free[t % 10] = ad
        if p + 2 < len(FFN_PASSES):
            for t in FFN_PASSES[p + 2][:2]:
                emit_reload(t)

    ffn_stage_a(0)
    for p, tiles in enumerate(FFN_PASSES):
        ffn_pass(p, tiles)
        sched.marks[f"p{p}"] = len(sched.items)

    E("gp", None, waits=[(s, s.n) for s in gd.sems if s.n > 0])
    E("sy", None, waits=[(s, s.n) for s in sd.sems if s.n > 0] +
      [(s, s.n) for s in gd.sems if s.n > 0])

    nc._marks = sched.marks
    nc._sched = sched
    if trunc is not None:
        sched.items = sched.items[:trunc]
    with nc.Block() as block:

        @block.tensor
        def _(e):
            sched.run("pe", e)

        @block.vector
        def _(e):
            sched.run("dv", e)

        @block.scalar
        def _(e):
            sched.run("ac", e)

        @block.gpsimd
        def _(e):
            sched.run({"gp", "po"}, e)

        @block.sync
        def _(e):
            sched.run("sy", e)

    ctx.close()
    return nc


def _host_inputs(L0_attn_x, L0_attn_Wq, L0_attn_bq, L0_attn_Wk, L0_attn_bk,
                 L0_attn_Wv, L0_attn_bv, L0_attn_Wo, L0_attn_bo,
                 L0_ffn_w1_w, L0_ffn_w1_s, L0_ffn_w2_w, L0_ffn_w2_s,
                 L0_ffn_h1_w, L0_ffn_h1_s, L0_ffn_h2_w, L0_ffn_h2_s,
                 L0_ffn_w3_w, L0_ffn_w3_s):
    f = np.float32

    def wfmt(w):
        return np.ascontiguousarray(
            np.asarray(w, f).reshape(NIC, 128, D).transpose(1, 0, 2).reshape(
                128, NIC * D))

    def colvec(v, n):
        return np.ascontiguousarray(np.asarray(v, f).reshape(n, 128).T)

    def blocked(wT, nb):
        K = wT.shape[0]
        r = wT.reshape(K // 128, 128, nb, 128)
        return np.ascontiguousarray(r.transpose(2, 1, 0, 3).reshape(nb, 128, K))

    w1T = np.ascontiguousarray(np.asarray(L0_ffn_w1_w, f).T)
    w2T = np.ascontiguousarray(np.asarray(L0_ffn_w2_w, f).T)
    h1T = np.ascontiguousarray(np.asarray(L0_ffn_h1_w, f).T)
    h2T = np.ascontiguousarray(np.asarray(L0_ffn_h2_w, f).T)
    w3T = np.ascontiguousarray(np.asarray(L0_ffn_w3_w, f).T)

    shared = {
        "wq": wfmt(L0_attn_Wq), "wk": wfmt(L0_attn_Wk),
        "wv": wfmt(L0_attn_Wv), "wo": wfmt(L0_attn_Wo),
        "bq": colvec(L0_attn_bq, NIC), "bk": colvec(L0_attn_bk, NIC),
        "bvb": np.ascontiguousarray(
            np.broadcast_to(np.asarray(L0_attn_bv, f)[None, :], (128, D))),
        "bob": np.ascontiguousarray(
            np.broadcast_to(np.asarray(L0_attn_bo, f)[None, :], (128, D))),
        "s1": colvec(L0_ffn_w1_s, NHC), "s2": colvec(L0_ffn_w2_s, NHC),
        "sh1": colvec(L0_ffn_h1_s, NHC), "sh2": colvec(L0_ffn_h2_s, NHC),
        "s3": colvec(L0_ffn_w3_s, NIC),
        "w12b": np.ascontiguousarray(np.concatenate(
            [blocked(w1T, NHC), blocked(w2T, NHC)], axis=2)),
        "h1b": blocked(h1T, NHC), "h2b": blocked(h2T, NHC),
        "w3b": blocked(w3T, NIC),
    }

    x = np.asarray(L0_attn_x, f)
    in_maps = []
    for b in range(B):
        for h in range(2):
            base = h * TOWN - HALO
            xe = np.zeros((TEXT, D), f)
            lo, hi = max(0, base), min(S, base + TEXT)
            xe[lo - base:hi - base] = x[b, lo:hi]
            wv = np.zeros((NW, 128), f)
            for j in range(NW):
                gs = base + 64 * j
                if 0 <= gs <= S - 128:
                    pidx = gs + np.arange(128)
                    cnt = np.where((pidx < 64) | (pidx >= S - 64), 1.0, 2.0)
                    wv[j] = 1.0 / cnt
            m = dict(shared)
            m["x_ext"] = xe
            m["wvec"] = np.ascontiguousarray(wv.T)
            in_maps.append(m)
    return in_maps


_NC_CACHE = []


def kernel(**inputs):
    in_maps = _host_inputs(**inputs)
    if not _NC_CACHE:
        _NC_CACHE.append(build_nc())
    nc = _NC_CACHE[0]
    res = run_bass_kernel_spmd(nc, in_maps, core_ids=list(range(NCORE)))
    y = np.empty((B, S, D), np.float32)
    for b in range(B):
        for h in range(2):
            y[b, h * TOWN:(h + 1) * TOWN] = np.asarray(
                res.results[b * 2 + h]["out"])
    return y


# version 27 (final)
# speedup vs baseline: 1.5207x; 1.5207x over previous
import numpy as np
import sys
from functools import reduce
from contextlib import ExitStack

import concourse.bass as bass
import concourse.mybir as mybir
from concourse.bass_utils import run_bass_kernel_spmd

F32 = mybir.dt.float32
BF16 = mybir.dt.bfloat16
AF = mybir.ActivationFunctionType
ALU = mybir.AluOpType
AX = mybir.AxisListType

B, S, D = 4, 8192, 1024
NCORE = 8
TOWN = 4096
HALO = 64
TEXT = TOWN + 2 * HALO
NT = TEXT // 128
NIC = D // 128
DH = 4096
NHC = DH // 128
NW = 65
GT = 256
GTE = GT + 64
NGRP = 17
EPS = 1e-5
TPMAX = 512
NTF = 32
FFN_PASSES = [list(range(4 * p, 4 * p + 4)) for p in range(8)]
AF_COPY = AF.Identity


def prod(xs):
    return reduce(lambda a, b: a * b, xs)


def rum(x, m):
    return (x + m - 1) // m * m


class Arena:
    def __init__(self, nc, size):
        self.nc = nc
        self.size = size
        self.handle = nc.alloc_sbuf_tensor("arena", [128, size], mybir.dt.uint8)
        self.base = nc.lookup_mloc(self.handle).addr
        self.top = 0
        self.hi = 0
        self._offs = {}

    def al(self, name, shape, dtype, at=None):
        per = prod(shape[1:]) * mybir.dt.size(dtype)
        off = self.top if at is None else at
        h = self.nc.alloc_sbuf_tensor_at(name, list(shape), dtype,
                                         offset=self.base + off)
        self._offs[id(h)] = off
        end = rum(off + per, 32)
        if at is None:
            self.top = end
        self.hi = max(self.hi, end)
        assert self.hi <= self.size, (name, self.hi, self.size)
        return h

    def mark(self):
        return self.top

    def reset(self, p):
        self.top = p

    def offset_of(self, h):
        return self._offs[id(h)]


class Ev:
    def __init__(self, h):
        self.h = h
        self.n = 0


class DmaPool:

    def __init__(self, sems):
        self.sems = sems
        self.i = 0


class Sched:
    def __init__(self):
        self.items = []

    def emit(self, eng, fn, waits=(), inc=None, inc_by=1):
        f = sys._getframe(1)
        self._tags = getattr(self, "_tags", [])
        self._tags.append((f.f_lineno, f.f_back.f_lineno if f.f_back else 0))
        ws = {}
        for w in waits:
            if w is None:
                continue
            ev, v = w
            if v <= 0:
                continue
            k = id(ev)
            ws[k] = (ev.h, max(v, ws[k][1]) if k in ws else v)
        if isinstance(inc, DmaPool):
            ev = inc.sems[inc.i % len(inc.sems)]
            inc.i += 1
            if ev.n > 0:
                ws[id(ev)] = (ev.h, ev.n)
            inc = ev
        rec = None
        if inc is not None:
            inc.n += inc_by
            rec = (inc, inc.n)
        self.items.append((eng, fn, list(ws.values()),
                           None if inc is None else (inc.h, inc_by)))
        return rec

    def run(self, name, eng):
        names = name if isinstance(name, (set, frozenset)) else {name}
        for en, fn, waits, inc in self.items:
            if en not in names:
                continue
            for h, v in waits:
                eng.wait_ge(h, v)
            inst = fn(eng) if fn is not None else None
            if inc is not None:
                h, n = inc
                inst.then_inc(h, n)


def windows_of_group():
    out = []
    last = -1
    for g in range(NGRP):
        hi = min(NW - 1, 4 * g + 2) if g < 16 else NW - 1
        out.append(list(range(last + 1, hi + 1)))
        last = hi
    return out


def out_tiles_of_group():
    out = [[0]]
    for g in range(1, 16):
        out.append([2 * g - 1, 2 * g])
    out.append([31, 32])
    return out


def qk_slice(j):
    g = j // 4
    return g, 64 * j - GT * g


def build_nc(trunc=None):
    nc = bass.Bass("TRN2", target_bir_lowering=False)

    x_ext = nc.dram_tensor("x_ext", [TEXT, D], F32, kind="ExternalInput").ap()
    wvec_in = nc.dram_tensor("wvec", [128, NW], F32, kind="ExternalInput").ap()
    w_in = {k: nc.dram_tensor(k, [128, NIC * D], F32, kind="ExternalInput").ap()
            for k in ("wq", "wk", "wv", "wo")}
    bq_in = nc.dram_tensor("bq", [128, NIC], F32, kind="ExternalInput").ap()
    bk_in = nc.dram_tensor("bk", [128, NIC], F32, kind="ExternalInput").ap()
    bv_in = nc.dram_tensor("bvb", [128, D], F32, kind="ExternalInput").ap()
    bo_in = nc.dram_tensor("bob", [128, D], F32, kind="ExternalInput").ap()
    s1_in = nc.dram_tensor("s1", [128, NHC], F32, kind="ExternalInput").ap()
    s2_in = nc.dram_tensor("s2", [128, NHC], F32, kind="ExternalInput").ap()
    sh1_in = nc.dram_tensor("sh1", [128, NHC], F32, kind="ExternalInput").ap()
    sh2_in = nc.dram_tensor("sh2", [128, NHC], F32, kind="ExternalInput").ap()
    s3_in = nc.dram_tensor("s3", [128, NIC], F32, kind="ExternalInput").ap()
    w12b = nc.dram_tensor("w12b", [NHC, 128, 2 * NIC * 128], F32,
                          kind="ExternalInput").ap()
    h1b = nc.dram_tensor("h1b", [NHC, 128, NHC * 128], F32, kind="ExternalInput").ap()
    h2b = nc.dram_tensor("h2b", [NHC, 128, NHC * 128], F32, kind="ExternalInput").ap()
    w3b = nc.dram_tensor("w3b", [NIC, 128, NHC * 128], F32, kind="ExternalInput").ap()
    out_h = nc.dram_tensor("out", [TOWN, D], F32, kind="ExternalOutput").ap()
    w12s = nc.dram_tensor("w12s", [NHC, 128, 2 * NIC * 128], BF16).ap()
    h1s = nc.dram_tensor("h1s", [NHC, 128, NHC * 128], BF16).ap()
    h2s = nc.dram_tensor("h2s", [NHC, 128, NHC * 128], BF16).ap()
    w3s = nc.dram_tensor("w3s", [NIC, 128, NHC * 128], BF16).ap()
    xms = nc.dram_tensor("xms", [TEXT, D], F32).ap()

    ar = Arena(nc, nc.sbuf_bytes_remaining - 1024)
    wvec = ar.al("p_wvec", [128, NW], F32)
    bq8 = ar.al("p_bq8", [128, NIC], F32)
    bk_s = ar.al("p_bk", [128, NIC], F32)
    bv_b = ar.al("p_bvb", [128, D], F32)
    bo_b = ar.al("p_bob", [128, D], F32)
    s1_s = ar.al("p_s1", [128, NHC], F32)
    s2_s = ar.al("p_s2", [128, NHC], F32)
    sh1_s = ar.al("p_sh1", [128, NHC], F32)
    sh2_s = ar.al("p_sh2", [128, NHC], F32)
    s3_s = ar.al("p_s3", [128, NIC], F32)
    id_f = ar.al("p_idf", [128, 128], F32)
    w12r = [ar.al(f"p_w12r{i}", [128, 2 * NIC * 128], BF16) for i in range(2)]
    wbig = [ar.al(f"p_wbig{i}", [128, NHC * 128], BF16) for i in range(2)]
    pmark = ar.mark()

    wq_s = ar.al("a_wq", [128, NIC * D], BF16)
    wk_s = ar.al("a_wk", [128, NIC * D], BF16)
    wv_s = ar.al("a_wv", [128, NIC * D], BF16)
    wo_s = ar.al("a_wo", [128, NIC * D], BF16)
    xr = [ar.al(f"a_x{i}", [128, D], F32) for i in range(7)]
    lntm = [ar.al(f"a_lntm{i}", [128, D], F32) for i in range(2)]
    lnfm = [ar.al(f"a_lnfm{i}", [128, NIC * GTE], BF16) for i in range(2)]
    qTb = [ar.al(f"a_qT{i}", [128, NIC * GTE], BF16) for i in range(2)]
    kTb = [ar.al(f"a_kT{i}", [128, NIC * GTE], BF16) for i in range(2)]
    vb = [ar.al(f"a_v{i}", [128, 2 * D], BF16) for i in range(2)]
    vsb = [ar.al(f"a_vs{i}", [128, D], BF16) for i in range(3)]
    outsb = [ar.al(f"a_out{i}", [128, D], BF16) for i in range(4)]
    probs = [ar.al(f"a_pr{i}", [128, 128], F32) for i in range(2)]
    attnT = [ar.al(f"a_aT{i}", [128, 128], BF16) for i in range(6)]
    attnsc = [ar.al(f"a_asc{i}", [128, 128], F32) for i in range(3)]
    xmst = [ar.al(f"a_xm{i}", [128, D], F32) for i in range(2)]
    wotmp = [ar.al(f"a_wot{i}", [128, D], F32) for i in range(2)]
    st12 = [ar.al(f"a_st12_{i}", [128, 12], F32) for i in range(2)]
    mvr = [ar.al(f"a_mv{i}", [128, 2], F32) for i in range(4)]
    rstdr = [ar.al(f"a_rstd{i}", [128, 1], F32) for i in range(4)]
    lnv = [ar.al(f"a_lnv{i}", [128, 1], F32) for i in range(2)]
    nmx = [ar.al(f"a_nmx{i}", [128, 1], F32) for i in range(2)]
    rsum = [ar.al(f"a_rs{i}", [128, 1], F32) for i in range(2)]
    rinv = [ar.al(f"a_ri{i}", [128, 1], F32) for i in range(2)]
    mfac = [ar.al(f"a_mf{i}", [128, 1], F32) for i in range(2)]
    veps = [ar.al(f"a_veps{i}", [128, 1], F32) for i in range(2)]
    attn_end = ar.mark()

    ar.reset(pmark)
    xmr = [ar.al(f"f_xm{i}", [128, D], F32) for i in range(10)]
    flntm = [ar.al(f"f_lntm{i}", [128, D], F32) for i in range(2)]
    flnfm = ar.al("f_lnfm", [128, NIC * TPMAX], BF16)
    aT = ar.al("f_aT", [128, NHC * TPMAX], BF16)
    bT = ar.al("f_bT", [128, NHC * TPMAX], BF16)
    out3T = ar.al("f_o3T", [128, NIC * TPMAX], F32, at=ar.offset_of(bT))
    usb = [ar.al(f"f_u{i}", [128, TPMAX], F32) for i in range(2)]
    ost = [ar.al(f"f_ost{i}", [128, D], F32) for i in range(2)]
    fst = [ar.al(f"f_st{i}", [128, 12], F32) for i in range(2)]
    fmv = [ar.al(f"f_mv{i}", [128, 2], F32) for i in range(2)]
    fveps = [ar.al(f"f_veps{i}", [128, 1], F32) for i in range(2)]
    flnv = [ar.al(f"f_lnv{i}", [128, 1], F32) for i in range(2)]
    frstd = [ar.al(f"f_rstd{i}", [128, 1], F32) for i in range(2)]
    ffn_end = ar.mark()

    cT = aT

    Z = [nc.alloc_psum_tensor(f"Z{i}", [128, 1024], F32) for i in range(4)]
    BANKS = [(Z[0], 0), (Z[0], 512), (Z[1], 0), (Z[1], 512)]
    SCORES = [Z[0][:, 0:128], Z[1][:, 512:640]]

    sched = Sched()
    ctx = ExitStack()
    pe = Ev(ctx.enter_context(nc.semaphore("s_pe")))
    dv = Ev(ctx.enter_context(nc.semaphore("s_dv")))
    ac = Ev(ctx.enter_context(nc.semaphore("s_ac")))
    po = Ev(ctx.enter_context(nc.semaphore("s_po")))
    gd = DmaPool([Ev(ctx.enter_context(nc.semaphore(f"s_g{i}")))
                  for i in range(32)])
    sd = DmaPool([Ev(ctx.enter_context(nc.semaphore(f"s_y{i}")))
                  for i in range(16)])
    E = sched.emit

    xle = [None] * NT
    xfree = [None] * 7

    def load_x(t):
        xle[t] = E("gp", lambda e, t=t: e.dma_start(
            out=xr[t % 7][:], in_=x_ext[t * 128:(t + 1) * 128, :]),
            waits=[xfree[t % 7]], inc=gd, inc_by=16)

    wcast = {}

    HW = NIC * D // 2

    def cast_w(name, dst):
        e0 = E("gp", lambda e, d=dst, s=w_in[name]:
               e.dma_start(out=d[:, 0:HW], in_=s[:, 0:HW]), inc=gd, inc_by=16)
        e1 = E("gp", lambda e, d=dst, s=w_in[name]:
               e.dma_start(out=d[:, HW:2 * HW], in_=s[:, HW:2 * HW]),
               inc=gd, inc_by=16)
        wcast[name] = (e0, e1)

    load_x(0)
    load_x(1)
    msev = E("gp", lambda e: e.memset(id_f[:], 0.0), inc=po)
    idev = E("gp", lambda e: e.affine_select(
        out=id_f[:], in_=id_f[:], compare_op=ALU.not_equal, fill=1.0, base=0,
        pattern=[[-1, 128]], channel_multiplier=1), waits=[msev], inc=po)
    sm = E("gp", lambda e: e.dma_start(out=bq8[:], in_=bq_in),
           inc=gd, inc_by=16)
    small_done = sm
    bq8ev = E("dv", lambda e: e.tensor_scalar_mul(bq8[:], bq8[:], 0.125),
              waits=[small_done], inc=dv)
    cast_w("wq", wq_s)
    cast_w("wk", wk_s)
    cast_w("wv", wv_s)
    load_x(2)
    load_x(3)
    cast_w("wo", wo_s)
    ld_ev = {}
    for nm, dst, src in [("wvec", wvec, wvec_in), ("bk", bk_s, bk_in),
                         ("bvb", bv_b, bv_in)]:
        ld_ev[nm] = E("gp", lambda e, d=dst, s=src:
                      e.dma_start(out=d[:], in_=s), inc=gd, inc_by=16)
    for nm, dst, src in [("bob", bo_b, bo_in), ("s1", s1_s, s1_in),
                         ("s2", s2_s, s2_in), ("sh1", sh1_s, sh1_in),
                         ("sh2", sh2_s, sh2_in), ("s3", s3_s, s3_in)]:
        ld_ev[nm] = E("gp", lambda e, d=dst, s=src:
                      e.dma_start(out=d[:], in_=s), inc=gd, inc_by=16)

    conv_jobs = []
    for name, src, dst, nb, step in (("w12", w12b, w12s, NHC, 8),
                                     ("h1", h1b, h1s, NHC, 4),
                                     ("h2", h2b, h2s, NHC, 4),
                                     ("w3", w3b, w3s, NIC, 4)):
        for lo in range(0, nb, step):
            conv_jobs.append((name, src, dst, lo, min(lo + step, nb), step))
    conv_ev = {}

    def emit_conv(job):
        name, src, dst, lo, hi, step = job
        ev = E("gp", lambda e, src=src, dst=dst, lo=lo, hi=hi:
               e.dma_start(out=dst[lo:hi], in_=src[lo:hi]), inc=gd, inc_by=16)
        conv_ev[(name, lo // step)] = ev

    for job in list(conv_jobs):
        if job[0] == "w12":
            emit_conv(job)
            conv_jobs.remove(job)

    WOG = windows_of_group()
    OTG = out_tiles_of_group()
    bank_free = [None] * 4
    P4 = [1, 2, 0, 3]
    P2 = [1, 2]
    p4_i = [0]
    p2_i = [0]

    def next4():
        b = P4[p4_i[0] % 4]
        p4_i[0] += 1
        return b

    def next2():
        b = P2[p2_i[0] % 2]
        p2_i[0] += 1
        return b
    zout_free = [None, None]
    lnfm_ev = [None, None]
    qk_ev = [None, None]
    v_ev = [None, None]
    outsb_ev = [None] * 4
    outsb_free = [None] * 4
    attnT_ev = [None] * 6
    attnT_free = [None] * 6
    probs_free = [None] * 2
    attnsc_free = [None] * 3
    asc_ev = [None] * 3
    vs_ev = [None] * 3
    vs_free = [None] * 3
    xm_free = [None] * 2
    exp_prev = [None]
    wotmp_free = [None] * 2
    scores_last_qk = {}
    out_last_v = {}
    lnfm_last_pe = {}
    xmid_spill = [None] * NT

    def next_proj():
        i = proj_i[0] % 2
        proj_i[0] += 1
        return i

    norm_ev = [None] * NT
    tptile_ev = [None] * NT

    def ln_stats(t):
        sl12 = st12[t % 2]
        mv = mvr[t % 4]
        rst = rstdr[t % 4]
        e1 = E("dv", lambda e, t=t, sl12=sl12: e.bn_stats(
            sl12[:, 0:6], xr[t % 7][:, 0:512]), waits=[xle[t]], inc=dv)
        e2 = E("dv", lambda e, t=t, sl12=sl12: e.bn_stats(
            sl12[:, 6:12], xr[t % 7][:, 512:1024]), waits=[e1], inc=dv)
        ea = E("dv", lambda e, mv=mv, sl12=sl12: e.bn_aggr(mv[:], sl12[:]),
               waits=[e2], inc=dv)
        ep = E("dv", lambda e, t=t, mv=mv: e.tensor_scalar_add(
            veps[t % 2][:], mv[:, 1:2], EPS), waits=[ea], inc=dv)
        el = E("ac", lambda e, t=t: e.activation(
            lnv[t % 2][:], veps[t % 2][:], AF.Ln, scale=1.0),
            waits=[ep], inc=ac)
        er = E("ac", lambda e, t=t, rst=rst: e.activation(
            rst[:], lnv[t % 2][:], AF.Exp, bias=0.0, scale=-0.5),
            waits=[el], inc=ac)
        norm_ev[t] = E("dv", lambda e, t=t, mv=mv, rst=rst: e.tensor_scalar(
            lntm[t % 2][:], xr[t % 7][:], mv[:, 0:1], rst[:],
            ALU.subtract, ALU.mult),
            waits=[er, tptile_ev[t - 2] if t >= 2 else None], inc=dv)

    def ln_transposes(t, g, lnfm_war):
        last = None
        for c in range(NIC):
            b = next4()
            zt, zo = BANKS[b]
            sl = zt[:, zo:zo + 128]
            it = (t - 2 * g) if g < 16 else 0
            tp = E("pe", lambda e, t=t, c=c, sl=sl: e.transpose(
                sl, lntm[t % 2][:, 128 * c:128 * c + 128], id_f[:]),
                waits=[norm_ev[t], idev, bank_free[b]], inc=pe)
            tptile_ev[t] = tp
            ev = E("dv", lambda e, g=g, c=c, it=it, sl=sl: e.tensor_copy(
                lnfm[g % 2][:, c * GTE + it * 128:c * GTE + it * 128 + 128], sl),
                waits=[tp, lnfm_war if (c == 0 and it == 0) else None], inc=dv)
            if it == 0 and g >= 1:
                ev = E("dv", lambda e, g=g, c=c, sl=sl: e.tensor_copy(
                    lnfm[(g - 1) % 2][:, c * GTE + GT:c * GTE + GT + 64],
                    sl[:, 0:64]), waits=[], inc=dv)
            bank_free[b] = ev
            last = ev
        return last

    def window_scores(j):
        wg, off = qk_slice(j)
        mmlast = None
        for ic in range(NIC):
            mmlast = E("pe", lambda e, ic=ic, wg=wg, off=off, j=j,
                       st=(ic == 0), sp=(ic == NIC - 1): e.matmul(
                SCORES[j % 2][:, 0:128],
                qTb[wg % 2][:, ic * GTE + off:ic * GTE + off + 128],
                kTb[wg % 2][:, ic * GTE + off:ic * GTE + off + 128],
                start=st, stop=sp),
                waits=([qk_ev[0], qk_ev[1], bank_free[0 if j % 2 == 0 else 3]]
                       if ic == 0 else []),
                inc=(pe if ic == NIC - 1 else None))
        scores_last_qk[wg] = mmlast
        nm = E("dv", lambda e, j=j: e.reduce_max(
            nmx[j % 2][:], SCORES[j % 2][:, 0:128], AX.X, negate=True),
            waits=[mmlast], inc=dv)
        ex = E("ac", lambda e, j=j: e.activation(
            probs[j % 2][:], SCORES[j % 2][:, 0:128], AF.Exp,
            bias=nmx[j % 2][:], scale=1.0, accum_out=rsum[j % 2][:]),
            waits=[nm, probs_free[j % 2]], inc=ac)
        bank_free[0 if j % 2 == 0 else 3] = ex
        ri = E("dv", lambda e, j=j: e.reciprocal(rinv[j % 2][:], rsum[j % 2][:]),
               waits=[ex], inc=dv)
        mf = E("dv", lambda e, j=j: e.tensor_mul(
            mfac[j % 2][:], rinv[j % 2][:], wvec[:, j:j + 1]),
            waits=[ri, ld_ev["wvec"]], inc=dv)
        asc = E("dv", lambda e, j=j: e.tensor_scalar_mul(
            attnsc[j % 3][:], probs[j % 2][:], mfac[j % 2][:]),
            waits=[mf, attnsc_free[j % 3]], inc=dv)
        probs_free[j % 2] = asc
        asc_ev[j % 3] = asc

    tp_ptr = [0]

    def window_transpose(j):
        b = next2()
        zt, zo = BANKS[b]
        tp = E("pe", lambda e, j=j, zt=zt, zo=zo: e.transpose(
            zt[:, zo:zo + 128], attnsc[j % 3][:], id_f[:]),
            waits=[asc_ev[j % 3], bank_free[b], idev], inc=pe)
        attnsc_free[j % 3] = tp
        at = E("dv", lambda e, j=j, zt=zt, zo=zo: e.tensor_copy(
            attnT[j % 6][:], zt[:, zo:zo + 128]),
            waits=[tp, attnT_free[j % 6]], inc=dv)
        bank_free[b] = at
        attnT_ev[j % 6] = at

    def transpose_upto(j):
        j = min(j, NW - 1)
        while tp_ptr[0] <= j:
            window_transpose(tp_ptr[0])
            tp_ptr[0] += 1

    def v_ap(u):
        return vb[(u // 2) % 2][:, (u % 2) * D:(u % 2) * D + D]

    def out_tile(u):
        zt = Z[2] if u % 2 == 0 else Z[3]
        jc, jl, jr = 2 * u, 2 * u - 1, 2 * u + 1
        have_l = jl >= 1
        have_r = jr <= NW - 1
        mms = []
        for c in range(NIC):
            mms.append((zt[:, 128 * c:128 * c + 128],
                        v_ap(u)[:, 128 * c:128 * c + 128],
                        attnT[jc % 6][:, 0:128], (c == 0 or c == 4)))
        if have_l:
            for c in range(NIC):
                mms.append((zt[:, 128 * c:128 * c + 64],
                            vsb[(jl // 2) % 3][:, 128 * c:128 * c + 128],
                            attnT[jl % 6][:, 64:128], False))
        if have_r:
            for c in range(NIC):
                mms.append((zt[:, 128 * c + 64:128 * c + 128],
                            vsb[(jr // 2) % 3][:, 128 * c:128 * c + 128],
                            attnT[jr % 6][:, 0:64], False))
        w0 = [attnT_ev[jc % 6], zout_free[u % 2], v_ev[(u // 2) % 2]]
        if have_l:
            w0 += [attnT_ev[jl % 6], vs_ev[(jl // 2) % 3]]
        if have_r:
            w0 += [attnT_ev[jr % 6], vs_ev[(jr // 2) % 3]]
        mmlast = None
        for i, (o, l, r, st) in enumerate(mms):
            mmlast = E("pe", lambda e, o=o, l=l, r=r, st=st,
                       sp=(i == len(mms) - 1): e.matmul(
                o, l, r, start=st, stop=sp, skip_group_check=True),
                waits=(w0 if i == 0 else []),
                inc=(pe if i == len(mms) - 1 else None))
        for gg in {(u - 1) // 2, u // 2, (u + 1) // 2}:
            if gg >= 0:
                out_last_v[gg] = mmlast
        if have_l:
            vs_free[(jl // 2) % 3] = mmlast
        if have_r:
            vs_free[(jr // 2) % 3] = mmlast
        ev = E("ac", lambda e, u=u, zt=zt: e.copy(outsb[u % 4][:], zt[:, 0:D]),
               waits=[mmlast, outsb_free[u % 4]], inc=ac)
        zout_free[u % 2] = ev
        outsb_ev[u % 4] = ev
        attnT_free[jc % 6] = mmlast
        if have_l:
            attnT_free[jl % 6] = mmlast
        if have_r:
            attnT_free[jr % 6] = mmlast

    def wo_tile(u):
        ev = None
        for oh in range(2):
            pi = next4()
            zt, zo = BANKS[pi]
            mmlast = None
            for ic in range(NIC):
                mmlast = E("pe", lambda e, ic=ic, oh=oh, zt=zt, zo=zo, u=u: e.matmul(
                    zt[:, zo:zo + 512],
                    outsb[u % 4][:, ic * 128:ic * 128 + 128],
                    wo_s[:, ic * D + oh * 512:ic * D + oh * 512 + 512],
                    start=(ic == 0), stop=(ic == NIC - 1)),
                    waits=([outsb_ev[u % 4], wcast["wo"][0], bank_free[pi]]
                           if ic == 0 else
                           ([wcast["wo"][1]] if ic == 4 else [])),
                    inc=(pe if ic == NIC - 1 else None))
            ev = E("dv", lambda e, oh=oh, zt=zt, zo=zo, u=u: e.scalar_tensor_tensor(
                wotmp[u % 2][:, oh * 512:oh * 512 + 512], zt[:, zo:zo + 512],
                1.0, bo_b[:, oh * 512:oh * 512 + 512], ALU.mult, ALU.add),
                waits=[mmlast, ld_ev["bob"],
                       wotmp_free[u % 2] if oh == 0 else None], inc=dv)
            bank_free[pi] = ev
        outsb_free[u % 4] = ev
        pl = E("po", lambda e, u=u: e.tensor_add(
            xmst[u % 2][:], wotmp[u % 2][:], xr[u % 7][:]),
            waits=[ev, xle[u],
                   xmid_spill[u - 2] if u >= 2 else None], inc=po)
        wotmp_free[u % 2] = pl
        xfree[u % 7] = pl
        sp = E("gp", lambda e, u=u: e.dma_start(
            out=xms[u * 128:(u + 1) * 128, :], in_=xmst[u % 2][:]),
            waits=[pl], inc=gd, inc_by=16)
        xmid_spill[u] = sp
        xm_free[u % 2] = sp

    def attn_group(g):
        tiles = [2 * g, 2 * g + 1] if g < 16 else [32]
        TT = 128 * len(tiles)
        for _ in range(2):
            if conv_jobs:
                emit_conv(conv_jobs.pop(0))
        prev_wo = OTG[g - 1] if g >= 1 else []
        if prev_wo:
            wo_tile(prev_wo[0])
        qk_war = scores_last_qk.get(g - 2)
        ev = None
        for kind, wsb, bias_ap, scale, dstb in (
                ("q", wq_s, bq8, 0.125, qTb), ("k", wk_s, bk_s, 1.0, kTb)):
            wev = wcast["w" + kind]
            for oc in range(NIC):
                pi = next4()
                zt, zo = BANKS[pi]
                mmlast = None
                for ic in range(NIC):
                    mmlast = E("pe", lambda e, ic=ic, oc=oc, zt=zt, zo=zo,
                               w=wsb, g=g, TT=TT: e.matmul(
                        zt[:, zo:zo + TT],
                        w[:, ic * D + oc * 128:ic * D + oc * 128 + 128],
                        lnfm[g % 2][:, ic * GTE:ic * GTE + TT],
                        start=(ic == 0), stop=(ic == NIC - 1)),
                        waits=([lnfm_ev[g % 2], wev[0], bank_free[pi],
                                qk_war if oc == 0 else None]
                               if ic == 0 else
                               ([wev[1]] if ic == 4 else [])),
                        inc=(pe if ic == NIC - 1 else None))
                ev = E("dv", lambda e, oc=oc, zt=zt, zo=zo, b=bias_ap,
                       sc=scale, db=dstb, g=g, TT=TT: e.tensor_scalar(
                    db[g % 2][:, oc * GTE:oc * GTE + TT], zt[:, zo:zo + TT],
                    sc, b[:, oc:oc + 1], ALU.mult, ALU.add),
                    waits=[mmlast, bq8ev, ld_ev["bk"]], inc=dv)
                if g >= 1:
                    ev = E("dv", lambda e, oc=oc, zt=zt, zo=zo, b=bias_ap,
                           sc=scale, db=dstb, g=g: e.tensor_scalar(
                        db[(g - 1) % 2][:, oc * GTE + GT:oc * GTE + GT + 64],
                        zt[:, zo:zo + 64],
                        sc, b[:, oc:oc + 1], ALU.mult, ALU.add),
                        waits=[], inc=dv)
                bank_free[pi] = ev
        qk_ev[g % 2] = ev
        for u in prev_wo[1:]:
            wo_tile(u)
        v_war = out_last_v.get(g - 2)

        def v_proj():
          for it, t in enumerate(tiles):
            for oh in range(2):
                pi = next4()
                zt, zo = BANKS[pi]
                mmlast = None
                for ic in range(NIC):
                    mmlast = E("pe", lambda e, ic=ic, it=it, oh=oh, zt=zt,
                               zo=zo, g=g: e.matmul(
                        zt[:, zo:zo + 512],
                        lnfm[g % 2][:, ic * GTE + it * 128:ic * GTE + it * 128 + 128],
                        wv_s[:, ic * D + oh * 512:ic * D + oh * 512 + 512],
                        start=(ic == 0), stop=(ic == NIC - 1)),
                        waits=([lnfm_ev[g % 2], wcast["wv"][0], bank_free[pi],
                                v_war if (it == 0 and oh == 0) else None]
                               if ic == 0 else
                               ([wcast["wv"][1]] if ic == 4 else [])),
                        inc=(pe if ic == NIC - 1 else None))
                ve = E("dv", lambda e, it=it, oh=oh, zt=zt, zo=zo, g=g: e.tensor_tensor(
                    vb[g % 2][:, it * D + oh * 512:it * D + oh * 512 + 512],
                    zt[:, zo:zo + 512], bv_b[:, oh * 512:oh * 512 + 512],
                    ALU.add), waits=[mmlast, ld_ev["bvb"]], inc=dv)
                bank_free[pi] = ve
          v_ev[g % 2] = ve

        def vs_proj():
          for w in ([4 * (g - 1) + 3] if g >= 1 else []) + \
                ([4 * g + 1] if (g < 16 and 4 * g + 1 <= NW - 1) else []):
            wg2 = w // 4
            off = 64 * w - GT * wg2
            slot = (w // 2) % 3
            for oh in range(2):
                pi = next4()
                zt, zo = BANKS[pi]
                mmlast = None
                for ic in range(NIC):
                    mmlast = E("pe", lambda e, ic=ic, oh=oh, zt=zt, zo=zo,
                               wg2=wg2, off=off: e.matmul(
                        zt[:, zo:zo + 512],
                        lnfm[wg2 % 2][:, ic * GTE + off:ic * GTE + off + 128],
                        wv_s[:, ic * D + oh * 512:ic * D + oh * 512 + 512],
                        start=(ic == 0), stop=(ic == NIC - 1)),
                        waits=([lnfm_ev[g % 2], wcast["wv"][0], bank_free[pi],
                                vs_free[slot] if oh == 0 else None]
                               if ic == 0 else
                               ([wcast["wv"][1]] if ic == 4 else [])),
                        inc=(pe if ic == NIC - 1 else None))
                ve2 = E("dv", lambda e, oh=oh, zt=zt, zo=zo, slot=slot: e.tensor_tensor(
                    vsb[slot][:, oh * 512:oh * 512 + 512],
                    zt[:, zo:zo + 512], bv_b[:, oh * 512:oh * 512 + 512],
                    ALU.add), waits=[mmlast, ld_ev["bvb"]], inc=dv)
                bank_free[pi] = ve2
            vs_ev[slot] = ve2

        wjs = WOG[g]
        for idx, j in enumerate(wjs):
            window_scores(j)
            if idx == min(1, len(wjs) - 1):
                v_proj()
                transpose_upto(j - 1)
        vs_proj()
        lnfm_last_pe[g] = (pe, pe.n)
        transpose_upto(wjs[-1] - 1)
        if OTG[g]:
            transpose_upto(2 * OTG[g][-1] + 1)
        for u in OTG[g]:
            out_tile(u)
        if g + 1 <= 16:
            lnfm_war = scores_last_qk.get(g - 1)
            evl = None
            for t in ([2 * (g + 1), 2 * (g + 1) + 1] if g + 1 < 16 else [32]):
                evl = ln_transposes(t, g + 1, lnfm_war)
            lnfm_ev[(g + 1) % 2] = evl
        gg = g + 2
        if gg <= 16:
            for t in ([2 * gg, 2 * gg + 1] if gg < 16 else [32]):
                load_x(t)
            for t in ([2 * gg, 2 * gg + 1] if gg < 16 else [32]):
                ln_stats(t)

    sched.marks = {"setup": len(sched.items)}
    for t in (0, 1):
        ln_stats(t)
    evl = None
    for t in (0, 1):
        evl = ln_transposes(t, 0, None)
    lnfm_ev[0] = evl
    for t in (2, 3):
        ln_stats(t)
    for g in range(NGRP):
        attn_group(g)
        sched.marks[f"g{g}"] = len(sched.items)
    for u in OTG[16]:
        wo_tile(u)
    sched.marks["attn"] = len(sched.items)

    bar = ([(pe, pe.n), (dv, dv.n), (ac, ac.n), (po, po.n)] +
           [(s, s.n) for s in gd.sems if s.n > 0])
    for en in ("pe", "dv", "ac", "po", "gp"):
        E(en, None, waits=bar)

    zfree = {0: None, 1: None, 2: None, 3: None}
    w12_free = [None] * 2
    wbig_free = [None] * 2
    wbig_i = [0]
    xmr_free = [None] * 10
    flnfm_last_pe = [None]
    aT_ready = [None]
    bT_ready = [None]
    cT_ready = [None]
    flnfm_ev_c = [None] * NIC
    aT_ev_c = [None] * NHC
    bT_ev_c = [None] * NHC
    cT_ev_c = [None] * NHC
    o3_ready = [None]
    bT_last_pe = [None]
    aT_last_pe = [None]
    usb_free = [None, None]
    ost_free = [None, None]
    reload_ev = [None] * NTF
    flntm_free = [None, None]
    fz4 = [None] * 4
    f_el_ring = [None, None]
    f_en_ring = [None, None]
    w12_pref = {}

    def emit_reload(t):
        if reload_ev[t] is None:
            r = t % 10
            reload_ev[t] = E("gp", lambda e, t=t, r=r: e.dma_start(
                out=xmr[r][:], in_=xms[HALO + t * 128:HALO + (t + 1) * 128, :]),
                waits=[xmid_spill[t], xmid_spill[t + 1], xmr_free[r]],
                inc=gd, inc_by=16)

    def ffn_stage_a(p):
        tiles = FFN_PASSES[p]
        if p >= 1:
            for sl in (0, 1):
                w12_pref[(p, sl)] = E("gp", lambda e, sl=sl: e.dma_start(
                    out=w12r[sl][:], in_=w12s[sl]),
                    waits=[conv_ev[("w12", 0)], w12_free[sl]],
                    inc=gd, inc_by=16)
        evl = None
        for k, t in enumerate(tiles):
            r = t % 10
            emit_reload(t)
            rl = reload_ev[t]
            s1 = E("dv", lambda e, r=r, k=k: e.bn_stats(
                fst[k % 2][:, 0:6], xmr[r][:, 0:512]), waits=[rl], inc=dv)
            s2 = E("dv", lambda e, r=r, k=k: e.bn_stats(
                fst[k % 2][:, 6:12], xmr[r][:, 512:1024]), waits=[s1], inc=dv)
            sa = E("dv", lambda e, k=k: e.bn_aggr(
                fmv[k % 2][:], fst[k % 2][:]), waits=[s2], inc=dv)
            sp_ = E("dv", lambda e, k=k: e.tensor_scalar_add(
                fveps[k % 2][:], fmv[k % 2][:, 1:2], EPS),
                waits=[sa, f_el_ring[k % 2]], inc=dv)
            el = E("ac", lambda e, k=k: e.activation(
                flnv[k % 2][:], fveps[k % 2][:], AF.Ln, scale=1.0),
                waits=[sp_], inc=ac)
            er = E("ac", lambda e, k=k: e.activation(
                frstd[k % 2][:], flnv[k % 2][:], AF.Exp, bias=0.0, scale=-0.5),
                waits=[el, f_en_ring[k % 2]], inc=ac)
            f_el_ring[k % 2] = el
            en = E("dv", lambda e, r=r, k=k: e.tensor_scalar(
                flntm[k % 2][:], xmr[r][:], fmv[k % 2][:, 0:1],
                frstd[k % 2][:], ALU.subtract, ALU.mult),
                waits=[er, flntm_free[k % 2]], inc=dv)
            f_en_ring[k % 2] = en
            tpf = None
            for c in range(NIC):
                zi = c % 2
                so = 512 * ((c // 2) % 2)
                sl = Z[zi][:, so:so + 128]
                tp = E("pe", lambda e, c=c, sl=sl, k=k: e.transpose(
                    sl, flntm[k % 2][:, 128 * c:128 * c + 128], id_f[:]),
                    waits=[en, zfree[zi] if (k == 0 and c < 4) else None,
                           fz4[c % 4]], inc=pe)
                ev = E("dv", lambda e, c=c, sl=sl, k=k: e.tensor_copy(
                    flnfm[:, c * TPMAX + k * 128:c * TPMAX + k * 128 + 128], sl),
                    waits=[tp, flnfm_last_pe[0] if (c == 0 and k == 0) else None],
                    inc=dv)
                zfree[zi] = ev
                fz4[c % 4] = ev
                evl = ev
                tpf = tp
                flnfm_ev_c[c] = ev
            flntm_free[k % 2] = tpf

    def ffn_pass(p, tiles):
        TP = 128 * len(tiles)
        N0 = min(512, TP)
        stt_ev = None
        for oc in range(NHC):
            sl1 = oc % 2
            sl2 = sl1
            d1 = w12_pref.pop((p, oc), None)
            if d1 is None:
                d1 = E("sy", lambda e, oc=oc, sl=sl1: e.dma_start(
                    out=w12r[sl][:], in_=w12s[oc]),
                    waits=[conv_ev[("w12", oc // 8)], w12_free[sl1]],
                    inc=sd, inc_by=16)
            d2 = d1
            z1 = Z[oc % 2]
            z2 = Z[2 + oc % 2]
            mm1 = mm2 = None
            for ic in range(NIC):
                mm1 = E("pe", lambda e, ic=ic, z1=z1, sl=sl1, N0=N0: e.matmul(
                    z1[:, 0:N0], w12r[sl][:, ic * 128:ic * 128 + 128],
                    flnfm[:, ic * TPMAX:ic * TPMAX + N0],
                    start=(ic == 0), stop=(ic == NIC - 1)),
                    waits=([d1, zfree[oc % 2]] if ic == 0 else []) +
                          ([flnfm_ev_c[ic]] if oc < 2 else []),
                    inc=(pe if (ic == NIC - 1 and TP <= 512) else None))
                if TP > 512:
                    mm1 = E("pe", lambda e, ic=ic, z1=z1, sl=sl1, TP=TP: e.matmul(
                        z1[:, 512:TP], w12r[sl][:, ic * 128:ic * 128 + 128],
                        flnfm[:, ic * TPMAX + 512:ic * TPMAX + TP],
                        start=(ic == 0), stop=(ic == NIC - 1)),
                        inc=(pe if ic == NIC - 1 else None))
            for ic in range(NIC):
                mm2 = E("pe", lambda e, ic=ic, z2=z2, sl=sl2, N0=N0: e.matmul(
                    z2[:, 0:N0],
                    w12r[sl][:, 1024 + ic * 128:1024 + ic * 128 + 128],
                    flnfm[:, ic * TPMAX:ic * TPMAX + N0],
                    start=(ic == 0), stop=(ic == NIC - 1)),
                    waits=([d2, zfree[2 + oc % 2]] if ic == 0 else []) +
                          ([flnfm_ev_c[ic]] if oc < 2 else []),
                    inc=(pe if (ic == NIC - 1 and TP <= 512) else None))
                if TP > 512:
                    mm2 = E("pe", lambda e, ic=ic, z2=z2, sl=sl2, TP=TP: e.matmul(
                        z2[:, 512:TP],
                        w12r[sl][:, 1024 + ic * 128:1024 + ic * 128 + 128],
                        flnfm[:, ic * TPMAX + 512:ic * TPMAX + TP],
                        start=(ic == 0), stop=(ic == NIC - 1)),
                        inc=(pe if ic == NIC - 1 else None))
            w12_free[sl1] = mm2
            su = E("ac", lambda e, oc=oc, z1=z1, TP=TP: e.activation(
                usb[oc % 2][:, 0:TP], z1[:, 0:TP], AF.Silu,
                scale=s1_s[:, oc:oc + 1]),
                waits=[mm1, usb_free[oc % 2]], inc=ac)
            zfree[oc % 2] = su
            st = E("dv", lambda e, oc=oc, z2=z2, TP=TP: e.scalar_tensor_tensor(
                aT[:, oc * TPMAX:oc * TPMAX + TP], z2[:, 0:TP],
                s2_s[:, oc:oc + 1], usb[oc % 2][:, 0:TP], ALU.mult, ALU.mult),
                waits=[mm2, su, aT_last_pe[0] if oc == 0 else None], inc=dv)
            zfree[2 + oc % 2] = st
            usb_free[oc % 2] = st
            aT_ev_c[oc] = st
            stt_ev = st
        aT_ready[0] = stt_ev
        flnfm_last_pe[0] = mm2
        for stage, hsrc, hkey, src, dst, scl, rdy, wdep in (
                ("h1", h1s, "h1", aT, bT, sh1_s, bT_ready, aT_ready),
                ("h2", h2s, "h2", bT, cT, sh2_s, cT_ready, bT_ready)):
            gel = None
            for pc in range(NHC):
                wslot = wbig_i[0] % 2
                wbig_i[0] += 1
                dd = E("sy", lambda e, pc=pc, hs=hsrc, ws=wslot: e.dma_start(
                    out=wbig[ws][:], in_=hs[pc]),
                    waits=[conv_ev[(hkey, pc // 4)], wbig_free[wslot]],
                    inc=sd, inc_by=16)
                zb = Z[pc % 2]
                mml = None
                cev = aT_ev_c if stage == "h1" else bT_ev_c
                for ic in range(NHC):
                    mml = E("pe", lambda e, ic=ic, zb=zb, ws=wslot, src=src,
                            N0=N0: e.matmul(
                        zb[:, 0:N0], wbig[ws][:, ic * 128:ic * 128 + 128],
                        src[:, ic * TPMAX:ic * TPMAX + N0],
                        start=(ic == 0), stop=(ic == NHC - 1)),
                        waits=([dd, zfree[pc % 2]] if ic == 0 else []) +
                              ([cev[ic]] if pc < 2 else []),
                        inc=(pe if (ic == NHC - 1 and TP <= 512) else None))
                    if TP > 512:
                        mml = E("pe", lambda e, ic=ic, zb=zb, ws=wslot,
                                src=src, TP=TP: e.matmul(
                            zb[:, 512:TP], wbig[ws][:, ic * 128:ic * 128 + 128],
                            src[:, ic * TPMAX + 512:ic * TPMAX + TP],
                            start=(ic == 0), stop=(ic == NHC - 1)),
                            inc=(pe if ic == NHC - 1 else None))
                wbig_free[wslot] = mml
                if stage == "h1":
                    aT_last_pe[0] = mml
                else:
                    bT_last_pe[0] = mml
                gel = E("ac", lambda e, pc=pc, zb=zb, dst=dst, scl=scl,
                        TP=TP: e.activation(
                    dst[:, pc * TPMAX:pc * TPMAX + TP], zb[:, 0:TP], AF.Gelu,
                    scale=scl[:, pc:pc + 1]),
                    waits=[mml, aT_last_pe[0] if (stage == "h2" and pc == 0)
                           else None], inc=ac)
                zfree[pc % 2] = gel
                (bT_ev_c if stage == "h1" else cT_ev_c)[pc] = gel
            rdy[0] = gel
        o3l = None
        for oc in range(NIC):
            wslot = wbig_i[0] % 2
            wbig_i[0] += 1
            dd = E("sy", lambda e, oc=oc, ws=wslot: e.dma_start(
                out=wbig[ws][:], in_=w3s[oc]),
                waits=[conv_ev[("w3", oc // 4)], wbig_free[wslot]],
                inc=sd, inc_by=16)
            zb = Z[2 + oc % 2]
            mml = None
            for hc in range(NHC):
                mml = E("pe", lambda e, hc=hc, zb=zb, ws=wslot, N0=N0: e.matmul(
                    zb[:, 0:N0], wbig[ws][:, hc * 128:hc * 128 + 128],
                    cT[:, hc * TPMAX:hc * TPMAX + N0],
                    start=(hc == 0), stop=(hc == NHC - 1)),
                    waits=([dd, zfree[2 + oc % 2]] if hc == 0 else []) +
                          ([cT_ev_c[hc]] if oc < 2 else []),
                    inc=(pe if (hc == NHC - 1 and TP <= 512) else None))
                if TP > 512:
                    mml = E("pe", lambda e, hc=hc, zb=zb, ws=wslot, TP=TP: e.matmul(
                        zb[:, 512:TP], wbig[ws][:, hc * 128:hc * 128 + 128],
                        cT[:, hc * TPMAX + 512:hc * TPMAX + TP],
                        start=(hc == 0), stop=(hc == NHC - 1)),
                        inc=(pe if hc == NHC - 1 else None))
            wbig_free[wslot] = mml
            aT_last_pe[0] = mml
            o3l = E("dv", lambda e, oc=oc, zb=zb, TP=TP: e.tensor_scalar_mul(
                out3T[:, oc * TPMAX:oc * TPMAX + TP], zb[:, 0:TP],
                s3_s[:, oc:oc + 1]),
                waits=[mml, bT_last_pe[0] if oc == 0 else None], inc=dv)
            zfree[2 + oc % 2] = o3l
        o3_ready[0] = o3l
        if p + 1 < len(FFN_PASSES):
            ffn_stage_a(p + 1)
        for k, t in enumerate(tiles):
            zb = Z[2 + k % 2]
            tpl = None
            for c in range(NIC):
                tpl = E("pe", lambda e, c=c, zb=zb, k=k: e.transpose(
                    zb[:, 128 * c:128 * c + 128],
                    out3T[:, c * TPMAX + k * 128:c * TPMAX + k * 128 + 128],
                    id_f[:]),
                    waits=([o3_ready[0], zfree[2 + k % 2]] if c == 0 else []),
                    inc=pe)
            ad = E("dv", lambda e, t=t, k=k, zb=zb: e.tensor_add(
                ost[k % 2][:], zb[:, 0:D], xmr[t % 10][:]),
                waits=[tpl, ost_free[k % 2]], inc=dv)
            zfree[2 + k % 2] = ad
            stv = E("gp", lambda e, t=t, k=k: e.dma_start(
                out=out_h[t * 128:(t + 1) * 128, :], in_=ost[k % 2][:]),
                waits=[ad], inc=gd, inc_by=16)
            ost_free[k % 2] = stv
            xmr_free[t % 10] = ad
        if p + 2 < len(FFN_PASSES):
            for t in FFN_PASSES[p + 2][:2]:
                emit_reload(t)

    ffn_stage_a(0)
    for p, tiles in enumerate(FFN_PASSES):
        ffn_pass(p, tiles)
        sched.marks[f"p{p}"] = len(sched.items)

    E("gp", None, waits=[(s, s.n) for s in gd.sems if s.n > 0])
    E("sy", None, waits=[(s, s.n) for s in sd.sems if s.n > 0] +
      [(s, s.n) for s in gd.sems if s.n > 0])

    nc._marks = sched.marks
    nc._sched = sched
    if trunc is not None:
        sched.items = sched.items[:trunc]
    with nc.Block() as block:

        @block.tensor
        def _(e):
            sched.run("pe", e)

        @block.vector
        def _(e):
            sched.run("dv", e)

        @block.scalar
        def _(e):
            sched.run("ac", e)

        @block.gpsimd
        def _(e):
            sched.run({"gp", "po"}, e)

        @block.sync
        def _(e):
            sched.run("sy", e)

    ctx.close()
    return nc


def _host_inputs(L0_attn_x, L0_attn_Wq, L0_attn_bq, L0_attn_Wk, L0_attn_bk,
                 L0_attn_Wv, L0_attn_bv, L0_attn_Wo, L0_attn_bo,
                 L0_ffn_w1_w, L0_ffn_w1_s, L0_ffn_w2_w, L0_ffn_w2_s,
                 L0_ffn_h1_w, L0_ffn_h1_s, L0_ffn_h2_w, L0_ffn_h2_s,
                 L0_ffn_w3_w, L0_ffn_w3_s):
    f = np.float32

    def wfmt(w):
        return np.ascontiguousarray(
            np.asarray(w, f).reshape(NIC, 128, D).transpose(1, 0, 2).reshape(
                128, NIC * D))

    def colvec(v, n):
        return np.ascontiguousarray(np.asarray(v, f).reshape(n, 128).T)

    def blocked(wT, nb):
        K = wT.shape[0]
        r = wT.reshape(K // 128, 128, nb, 128)
        return np.ascontiguousarray(r.transpose(2, 1, 0, 3).reshape(nb, 128, K))

    w1T = np.ascontiguousarray(np.asarray(L0_ffn_w1_w, f).T)
    w2T = np.ascontiguousarray(np.asarray(L0_ffn_w2_w, f).T)
    h1T = np.ascontiguousarray(np.asarray(L0_ffn_h1_w, f).T)
    h2T = np.ascontiguousarray(np.asarray(L0_ffn_h2_w, f).T)
    w3T = np.ascontiguousarray(np.asarray(L0_ffn_w3_w, f).T)

    shared = {
        "wq": wfmt(L0_attn_Wq), "wk": wfmt(L0_attn_Wk),
        "wv": wfmt(L0_attn_Wv), "wo": wfmt(L0_attn_Wo),
        "bq": colvec(L0_attn_bq, NIC), "bk": colvec(L0_attn_bk, NIC),
        "bvb": np.ascontiguousarray(
            np.broadcast_to(np.asarray(L0_attn_bv, f)[None, :], (128, D))),
        "bob": np.ascontiguousarray(
            np.broadcast_to(np.asarray(L0_attn_bo, f)[None, :], (128, D))),
        "s1": colvec(L0_ffn_w1_s, NHC), "s2": colvec(L0_ffn_w2_s, NHC),
        "sh1": colvec(L0_ffn_h1_s, NHC), "sh2": colvec(L0_ffn_h2_s, NHC),
        "s3": colvec(L0_ffn_w3_s, NIC),
        "w12b": np.ascontiguousarray(np.concatenate(
            [blocked(w1T, NHC), blocked(w2T, NHC)], axis=2)),
        "h1b": blocked(h1T, NHC), "h2b": blocked(h2T, NHC),
        "w3b": blocked(w3T, NIC),
    }

    x = np.asarray(L0_attn_x, f)
    in_maps = []
    for b in range(B):
        for h in range(2):
            base = h * TOWN - HALO
            xe = np.zeros((TEXT, D), f)
            lo, hi = max(0, base), min(S, base + TEXT)
            xe[lo - base:hi - base] = x[b, lo:hi]
            wv = np.zeros((NW, 128), f)
            for j in range(NW):
                gs = base + 64 * j
                if 0 <= gs <= S - 128:
                    pidx = gs + np.arange(128)
                    cnt = np.where((pidx < 64) | (pidx >= S - 64), 1.0, 2.0)
                    wv[j] = 1.0 / cnt
            m = dict(shared)
            m["x_ext"] = xe
            m["wvec"] = np.ascontiguousarray(wv.T)
            in_maps.append(m)
    return in_maps


_NC_CACHE = []


def kernel(**inputs):
    in_maps = _host_inputs(**inputs)
    if not _NC_CACHE:
        _NC_CACHE.append(build_nc())
    nc = _NC_CACHE[0]
    res = run_bass_kernel_spmd(nc, in_maps, core_ids=list(range(NCORE)))
    y = np.empty((B, S, D), np.float32)
    for b in range(B):
        for h in range(2):
            y[b, h * TOWN:(h + 1) * TOWN] = np.asarray(
                res.results[b * 2 + h]["out"])
    return y
